# version 09
# baseline (speedup 1.0000x reference)
import contextlib

import numpy as np

import concourse.bass as bass
import concourse.mybir as mybir
from concourse.bass_utils import run_bass_kernel_spmd

F32 = mybir.dt.float32
BF16 = mybir.dt.bfloat16
AF = mybir.ActivationFunctionType
OP = mybir.AluOpType

T = 2048
D = 1024
NCH = 8
MC = 256
NMC = T // MC
MU = 128
NMU = MC // MU
NMUG = NMC * NMU
H = 8
SCALE = float(128) ** -0.5
EPS = 1e-5

WEIGHT_SPECS = [
    ("wq", 8, 1024, False),
    ("wg1", 8, 128, False),
    ("wg2", 1, 1024, False),
    ("wk", 8, 1024, False),
    ("wv", 8, 1024, False),
    ("wo", 8, 1024, True),
    ("wr", 8, 1024, False),
    ("wk2", 8, 1024, False),
    ("wv2", 8, 1024, False),
]

M_SA, M_CA1, M_GA, M_SB, M_CB1, M_GB, M_MUR, M_MUK = 0, 8, 16, 24, 32, 40, 48, 56


def build_nc(nmc=NMC, dump=(), attn_only=False, cutoff=None, wlimit=None, wmode='both'):
    nc = bass.Bass("TRN2", target_bir_lowering=False, detect_race_conditions=False)
    stack = contextlib.ExitStack()

    dr = {"x": nc.dram_tensor("x", [D, T], F32, kind="ExternalInput").ap()}
    for nm, nch, dout, _ in WEIGHT_SPECS:
        dr[nm] = nc.dram_tensor(nm, [nch * 128, dout], F32, kind="ExternalInput").ap()
    dr["wmod"] = nc.dram_tensor("wmod", [1024, 6144], F32, kind="ExternalInput").ap()
    for nm in ("cvec", "gnw", "mur", "muk"):
        dr[nm] = nc.dram_tensor(nm, [128, 8], F32, kind="ExternalInput").ap()
    dr["msk"] = nc.dram_tensor("msk", [128, 128], F32, kind="ExternalInput").ap()
    dr["y"] = nc.dram_tensor("y", [D, T], F32, kind="ExternalOutput").ap()

    ST = {"T": [], "V": [], "S": [], "Y": [], "G": []}
    CNT = {k: 0 for k in ("T", "V", "S", "GP", "DV", "DW0", "DW1",
                          "DX0", "DX1", "DO0", "DO1",
                          "Wwq", "Wwg1", "Wwg2", "Wwk", "Wwv", "Wwo",
                          "Wwr", "Wwk2", "Wwv2")}
    SEMS = {k: stack.enter_context(nc.semaphore("sem" + k)) for k in CNT}

    def ew(eng, waits, build, inc=None, inc_by=1):
        if eng == "GP":
            eng = "G"
            if inc == "GP" or inc is None:
                inc = inc
        for s, v in waits:
            assert v <= CNT[s], f"wait on future {s}:{v} > {CNT[s]}"

        def op(e, _w=tuple(waits), _b=build, _i=inc, _n=inc_by):
            for s, v in _w:
                if v > 0:
                    e.wait_ge(SEMS[s], v)
            ins = _b(e) if _b is not None else None
            if _i is not None:
                ins.then_inc(SEMS[_i], _n)
        ST[eng].append(op)
        if inc is not None:
            CNT[inc] += inc_by
            return CNT[inc]
        return None

    def sbuf(name, shape, dt):
        return stack.enter_context(nc.sbuf_tensor(name, shape, dt))

    def psum(name, shape):
        return stack.enter_context(nc.psum_tensor(name, shape, F32))

    xin = [sbuf("xin0", [128, T], F32), sbuf("xin1", [128, T], F32)]
    xmid = sbuf("xmid", [128, T], F32)
    stage_bf = [sbuf("stagebf0", [128, 1024], BF16), sbuf("stagebf1", [128, 1024], BF16)]
    wb = {nm: sbuf(nm + "_b", [128, nch * dout], BF16)
          for nm, nch, dout, _ in WEIGHT_SPECS}

    h_t = sbuf("h_t", [128, T], BF16)
    h_u = sbuf("h_u", [128, T], BF16)
    qt_t = sbuf("qt_t", [128, T], BF16)
    qf_t = sbuf("qf_t", [128, T], BF16)
    ufm_t = sbuf("ufm_t", [128, T], BF16)
    utm_t = sbuf("utm_t", [128, T], BF16)
    vtm_t = sbuf("vtm_t", [128, T], BF16)
    sg_t = sbuf("sg_t", [128, T], BF16)
    or_t = sbuf("or_t", [128, T], BF16)
    atm_t = sbuf("atm_t", [128, T], BF16)
    ssb_t = sbuf("ssb_t", [128, 1024], BF16)
    g1_t = sbuf("g1_t", [128, MC], BF16)
    sscr = sbuf("sscr", [128, 2 * MC], BF16)
    krl = sbuf("krl", [128, 2 * MC], BF16)
    obuf = sbuf("obuf", [128, MC], BF16)
    ones256 = sbuf("ones256", [128, MC], BF16)
    silc_bf = sbuf("silc_bf", [128, 8], BF16)
    h2car = sbuf("h2car", [128, 8], BF16)

    E_s = sbuf("E_s", [128, 2 * MC], F32)
    eA = sbuf("eA", [128, 2 * MC], F32)
    eR = sbuf("eR", [128, 2 * MC], F32)
    sM = sbuf("sM", [128, MC], F32)
    sA = sbuf("sA", [128, MC], F32)
    sB = sbuf("sB", [128, MC], F32)
    sC = sbuf("sC", [128, MC], F32)
    sR = sbuf("sR", [128, MC], F32)
    f0 = sbuf("f0", [128, MC], F32)
    gf0 = sbuf("gf0", [128, MC], F32)
    gf1 = sbuf("gf1", [128, MC], F32)
    gob = sbuf("gob", [128, MC], BF16)
    f1 = sbuf("f1", [128, MC], F32)
    f2 = sbuf("f2", [128, MC], F32)
    modv = sbuf("modv", [128, 64], F32)
    mtmp = sbuf("mtmp", [128, 24], F32)
    silc = sbuf("silc", [128, 8], F32)
    cv_s = sbuf("cv_s", [128, 8], F32)
    gnw_s = sbuf("gnw_s", [128, 8], F32)
    mur_s = sbuf("mur_s", [128, 8], F32)
    muk_s = sbuf("muk_s", [128, 8], F32)
    ecar = sbuf("ecar", [128, 8], F32)
    msk_s = sbuf("msk_s", [128, 128], F32)

    pS = psum("pS", [128, 1024])
    pX = psum("pX", [128, 1024])
    pO = psum("pO", [128, 1024])
    pPa = psum("pPa", [128, 512])
    pPb = psum("pPb", [128, 512])

    RD = {k: [] for k in ("pS", "pX", "pO", "pO0", "pO1", "pPa", "pPb",
                          "sscr0", "sscr1", "E_s", "atm0", "atm1")}

    def mm(out, lhsT, rhs, start, stop):
        return lambda e: e.matmul(out, lhsT, rhs, start=start, stop=stop,
                                  skip_group_check=True)

    ew("V", [], lambda e: e.memset(ones256[:, :], 1.0))
    ew("V", [], lambda e: e.memset(h2car[:, :], 0.0))

    for nm, dst in (("cvec", cv_s), ("gnw", gnw_s), ("mur", mur_s),
                    ("muk", muk_s), ("msk", msk_s)):
        ew("Y", [], lambda e, _d=dst, _n=nm: e.dma_start(out=_d[:, :], in_=dr[_n]),
           inc="DV", inc_by=16)
    dw_vecs = CNT["DV"]

    ew("S", [("DV", dw_vecs)],
       lambda e: e.activation(silc[:, :], cv_s[:, :], AF.Sigmoid), inc="S")
    vme = ew("V", [("S", CNT["S"])],
             lambda e: e.tensor_mul(silc[:, :], silc[:, :], cv_s[:, :]), inc="V")
    ew("V", [("V", vme)], lambda e: e.tensor_copy(silc_bf[:, :], silc[:, :]))
    ew("V", [], lambda e: e.tensor_scalar_add(modv[:, M_MUR:M_MUR + 8], mur_s[:, :], -1.0))
    ew("V", [], lambda e: e.tensor_scalar_add(modv[:, M_MUK:M_MUK + 8], muk_s[:, :], -1.0),
       inc="V")
    v_silc = CNT["V"]

    def emit_xin(mc):
        w = []
        if mc >= 2:
            w = [("DO%d" % (mc % 2), 16 * ((mc - 2) // 2 + 1))]
        ew("Y", w,
           lambda e, _mc=mc: e.dma_start(
               out=xin[_mc % 2][:, :].rearrange("p (c t) -> p c t", c=NCH),
               in_=dr["x"].rearrange("(c p) t -> p c t", p=128)[:, :, _mc * MC:(_mc + 1) * MC]),
           inc="DX%d" % (mc % 2), inc_by=16)

    emit_xin(0)
    if nmc > 1:
        emit_xin(1)

    wm_mm, wm_cast = {}, {}
    for g in range(6):
        for kc in range(8):
            i = g * 8 + kc
            xh = xmid[:, (i % 2) * 1024:(i % 2 + 1) * 1024]
            w = [("V", wm_cast[i - 2])] if i >= 2 else []
            dwk = "DW%d" % (i % 2)
            ew("Y", w,
               lambda e, _i=i, _g=g, _kc=kc, _x=xh: e.dma_start(
                   out=_x,
                   in_=dr["wmod"][_kc * 128:(_kc + 1) * 128, _g * 1024:(_g + 1) * 1024]),
               inc=dwk, inc_by=16)
            w = [(dwk, CNT[dwk])] + ([("T", wm_mm[i - 2])] if i >= 2 else [])
            wm_cast[i] = ew("V", w,
                            lambda e, _i=i, _x=xh: e.tensor_copy(
                                stage_bf[_i % 2][:, :], _x), inc="V")
            for j in range(8):
                w = [("V", wm_cast[i])] if j == 0 else []
                if i == 0 and j == 0:
                    w.append(("V", v_silc))
                col = kc * 48 + g * 8 + j
                ew("T", w,
                   mm(pX[:, col:col + 1],
                      stage_bf[i % 2][:, j * 128:(j + 1) * 128],
                      silc_bf[:, kc:kc + 1],
                      start=True, stop=True),
                   inc=("T" if j == 7 else None))
            wm_mm[i] = CNT["T"]
    t_mod = CNT["T"]

    def vspacer():
        ew("V", [], lambda e: e.memset(f1[:, :], 0.0))
        ew("V", [], lambda e: e.memset(f2[:, :], 0.0))
    ew("V", [("T", t_mod)], lambda e: e.memset(f1[:, :], 0.0))
    for _ in range(4):
        vspacer()
    vp = ew("V", [], lambda e: e.tensor_copy(
        f0[:, 0:48], pX[:, 0:48]), inc="V")
    for kc in range(1, 8):
        vspacer()
        vp = ew("V", [("V", vp)], lambda e, _k=kc: e.tensor_add(
            f0[:, 0:48], f0[:, 0:48], pX[:, _k * 48:(_k + 1) * 48]), inc="V")
    for _ in range(4):
        vspacer()
    ew("V", [("V", vp)], lambda e: e.tensor_copy(modv[:, M_SA:M_SA + 8], f0[:, 0:8]))
    ew("V", [], lambda e: e.tensor_scalar_add(modv[:, M_CA1:M_CA1 + 8], f0[:, 8:16], 1.0))
    ew("V", [], lambda e: e.tensor_copy(modv[:, M_SB:M_SB + 8], f0[:, 24:32]))
    ew("V", [], lambda e: e.tensor_scalar_add(modv[:, M_CB1:M_CB1 + 8], f0[:, 32:40], 1.0))
    v_f0 = ew("V", [], lambda e: e.tensor_copy(mtmp[:, 16:24], f0[:, 16:24]), inc="V")
    for src_, dst, mo in ((40, M_GB, 8), (16, M_GA, 0)):
        ew("S", [("V", v_f0)], lambda e, _s=src_, _m=mo: e.activation(
            mtmp[:, _m:_m + 8], f0[:, _s:_s + 8], AF.Sign))
        s_sg = ew("S", [], lambda e, _s=src_, _d=dst: e.activation(
            modv[:, _d:_d + 8], f0[:, _s:_s + 8], AF.Abs), inc="S")
        v_am = ew("V", [("S", s_sg)], lambda e, _d=dst: e.tensor_scalar_max(
            modv[:, _d:_d + 8], modv[:, _d:_d + 8], 0.25), inc="V")
        ew("V", [("V", v_am)], lambda e, _d=dst, _m=mo: e.tensor_mul(
            modv[:, _d:_d + 8], modv[:, _d:_d + 8], mtmp[:, _m:_m + 8]), inc="V")
    ew("V", [], lambda e: e.tensor_copy(mtmp[:, 16:24], f0[:, 40:48]), inc="V")
    RD["pX"] = [("V", CNT["V"])]

    def early_exit():
        name2buf = {"modv": modv, "mtmp": mtmp, "silc": silc, "silc_bf": silc_bf,
                    "cv_s": cv_s, "sM": sM, "sR": sR, "xin0": xin[0],
                    "stagebf0": stage_bf[0], "stagebf1": stage_bf[1],
                    "ones256": ones256, "E_s": E_s, "ecar": ecar, "g1_t": g1_t,
                    "h_t": h_t, "qt_t": qt_t, "qf_t": qf_t, "ufm_t": ufm_t,
                    "utm_t": utm_t, "vtm_t": vtm_t, "sg_t": sg_t, "or_t": or_t,
                    "atm_t": atm_t, "ssb_t": ssb_t, "xmid": xmid}
        w = [("T", CNT["T"]), ("V", CNT["V"]), ("S", CNT["S"])]
        dcnt = 0
        for di, nm in enumerate(dump):
            buf = name2buf[nm]
            dd = nc.dram_tensor("dbg_" + nm, list(buf.ap().shape),
                                buf.ap().dtype, kind="ExternalOutput").ap()
            ew("G", (w if di == 0 else []),
               lambda e, _d=dd, _b=buf: e.dma_start(out=_d, in_=_b[:, :]),
               inc="DO0", inc_by=16)
            dcnt += 1
        ew("G", [], lambda e: e.dma_start(
            out=dr["y"].rearrange("(c p) t -> p c t", p=128)[:, :, 0:MC],
            in_=xin[0][:, 0:NCH * MC].rearrange("p (c t) -> p c t", c=NCH)),
           inc="DO0", inc_by=16)
        ew("G", [("DO0", CNT["DO0"])], None)
        with nc.Block() as block:
            @block.tensor
            def _(e):
                for op in ST["T"]:
                    op(e)

            @block.vector
            def _(e):
                for op in ST["V"]:
                    op(e)

            @block.scalar
            def _(e):
                for op in ST["S"]:
                    op(e)

            @block.sync
            def _(e):
                for op in ST["Y"]:
                    op(e)

            @block.gpsimd
            def _(e):
                for op in ST["G"]:
                    op(e)
        stack.close()
        return nc

    if cutoff == "mod":
        return early_exit()

    wdone = {}
    for nm, nch, dout, fold in (WEIGHT_SPECS if wlimit is None else WEIGHT_SPECS[:wlimit]):
        wk_sem = "W" + nm
        for kc in range(nch):
            ew("G", [],
               lambda e, _nm=nm, _kc=kc, _d=dout: e.dma_start(
                   out=wb[_nm][:, _kc * _d:(_kc + 1) * _d],
                   in_=dr[_nm][_kc * 128:(_kc + 1) * 128, :]),
               inc=wk_sem, inc_by=16)
        if fold:
            for kc in range(nch):
                ew("V", ([(wk_sem, 16 * nch)] if kc == 0 else []),
                   lambda e, _nm=nm, _kc=kc, _d=dout: e.tensor_scalar(
                       wb[_nm][:, _kc * _d:(_kc + 1) * _d],
                       wb[_nm][:, _kc * _d:(_kc + 1) * _d],
                       gnw_s[:, _kc:_kc + 1], None, OP.mult),
                   inc=("V" if kc == nch - 1 else None))
            wdone[nm] = ("V", CNT["V"])
        else:
            wdone[nm] = (wk_sem, 16 * nch)

    if cutoff == "weights":
        return early_exit()

    o_dma_cnt = 0
    v_final = 0

    def emit_xin(mc):
        w = [("T", t_mod)] if mc < 2 else []
        if mc >= 2:
            w = [("DO%d" % (mc % 2), 16 * ((mc - 2) // 2 + 1))]
        ew("Y", w,
           lambda e, _mc=mc: e.dma_start(
               out=xin[_mc % 2][:, :].rearrange("p (c t) -> p c t", c=NCH),
               in_=dr["x"].rearrange("(c p) t -> p c t", p=128)[:, :, _mc * MC:(_mc + 1) * MC]),
           inc="DX%d" % (mc % 2), inc_by=16)

    emit_xin(0)
    if nmc > 1:
        emit_xin(1)

    def proj(wname, dout, rhs_t, m, bank, bkey, extra_w, evac):
        nch = next(s[1] for s in WEIGHT_SPECS if s[0] == wname)
        for kc in range(nch):
            w = []
            if kc == 0:
                w = list(RD[bkey]) + extra_w
                RD[bkey] = []
            ew("T", w,
               mm(bank[:, 0:MC],
                  wb[wname][:, kc * dout + m * 128:kc * dout + (m + 1) * 128],
                  rhs_t[:, kc * MC:(kc + 1) * MC],
                  start=(kc == 0), stop=(kc == nch - 1)),
               inc=("T" if kc == nch - 1 else None))
        t_end = CNT["T"]
        evac(t_end, bank, bkey)
        return t_end

    ctx = {i: {} for i in range(nmc)}
    banks4 = [(pPa, "pPa"), (pPb, "pPb"),
              (pO[:, 0:512], "pO0"), (pO[:, 512:1024], "pO1")]
    pP = (pPa, pPb)

    def hb(i):
        return h_t if i % 2 == 0 else h_u

    def spill_pO():
        if RD["pO"]:
            RD["pO0"] = RD["pO0"] + RD["pO"]
            RD["pO1"] = RD["pO1"] + RD["pO"]
            RD["pO"] = []

    def dl_ap(c, lo, hi):
        return stage_bf[c // 4][:, (c % 4) * MC + lo:(c % 4) * MC + hi]

    def ln_stats_front(xsrc, xtra, pa, pb, ka, kb, sq):
        ew("V", xtra, lambda e, _x=xsrc: e.tensor_copy(atm_t[:, :], _x[:, :]))
        ew("V", [], lambda e, _q=sq: e.tensor_mul(_q[:, :], atm_t[:, :], atm_t[:, :]),
           inc="V")
        v_c = CNT["V"]
        for kc in range(8):
            w = ([("V", v_c)] + RD[ka]) if kc == 0 else []
            if kc == 0:
                RD[ka] = []
            ew("T", w, mm(pa, ones256[:, 0:128],
                          atm_t[:, kc * MC:(kc + 1) * MC],
                          start=(kc == 0), stop=(kc == 7)))
        for kc in range(8):
            w = list(RD[kb]) if kc == 0 else []
            if kc == 0:
                RD[kb] = []
            ew("T", w, mm(pb, ones256[:, 0:128],
                          sq[:, kc * MC:(kc + 1) * MC],
                          start=(kc == 0), stop=(kc == 7)),
               inc=("T" if kc == 7 else None))
        return CNT["T"]

    def ln_stats_back(t_st, pa, pb, ka, kb):
        ew("V", [("T", t_st), ("GP", CNT["GP"])],
           lambda e, _p=pa: e.tensor_scalar_mul(sM[:, :], _p, 1.0 / D))
        ew("V", [], lambda e, _p=pb: e.tensor_scalar_mul(sA[:, :], _p, 1.0 / D))
        ew("V", [], lambda e: e.tensor_mul(sB[:, :], sM[:, :], sM[:, :]))
        ew("V", [], lambda e: e.tensor_sub(sC[:, :], sA[:, :], sB[:, :]), inc="V")
        v_st = CNT["V"]
        RD[ka] = [("V", v_st)]
        RD[kb] = [("V", v_st)]
        ew("S", [("V", v_st)], lambda e: e.activation(sA[:, :], sC[:, :], AF.Ln,
                                                      bias=np.float32(EPS)))
        s_r = ew("S", [], lambda e: e.activation(sR[:, :], sA[:, :], AF.Exp,
                                                 scale=np.float32(-0.5)), inc="S")
        return s_r

    def ln_apply(xsrc, hdst, s_rstd, scl_off, add_off):
        gsub = {}
        for kc in (5, 6, 7):
            dst = {5: gf0, 6: gf1, 7: f2}[kc]
            w = [("S", s_rstd)] if kc == 5 else []
            gsub[kc] = ew("GP", w, lambda e, _k=kc, _x=xsrc, _d=dst: e.tensor_sub(
                _d[:, :], _x[:, _k * MC:(_k + 1) * MC], sM[:, :]), inc="GP")
        for kc in range(8):
            if kc < 5:
                w = [("S", s_rstd)] if kc == 0 else []
                ew("V", w, lambda e, _k=kc, _x=xsrc: e.tensor_sub(
                    f0[:, :], _x[:, _k * MC:(_k + 1) * MC], sM[:, :]))
                src_, w2 = f0, []
            else:
                src_ = {5: gf0, 6: gf1, 7: f2}[kc]
                w2 = [("GP", gsub[kc])]
            ew("V", w2, lambda e, _k=kc, _s=scl_off, _sc=src_: e.scalar_tensor_tensor(
                f1[:, :], _sc[:, :], modv[:, _s + _k:_s + _k + 1], sR[:, :],
                OP.mult, OP.mult))
            ew("V", [], lambda e, _k=kc, _a=add_off, _h=hdst: e.tensor_scalar(
                _h[:, _k * MC:(_k + 1) * MC], f1[:, :],
                modv[:, _a + _k:_a + _k + 1], None, OP.add),
               inc=("V" if kc == 7 else None))
        return ("V", CNT["V"])

    def phA_front(i):
        c = ctx[i]
        xa = xin[i % 2]
        xtra = [("DX%d" % (i % 2), 16 * (i // 2 + 1))]
        if i >= 1:
            xtra += list(RD["pX"])
            RD["pX"] = []
        c["t_statsA"] = ln_stats_front(xa, xtra, pX[:, 0:MC], pX[:, 512:512 + MC],
                                       "pX", "pX", qt_t)

    def phA_back(i):
        c = ctx[i]
        xa = xin[i % 2]
        s_r = ln_stats_back(c["t_statsA"], pX[:, 0:MC], pX[:, 512:512 + MC],
                            "pX", "pX")
        c["vh"] = ln_apply(xa, hb(i), s_r, M_CA1, M_SA)

    def phB1(i):
        c = ctx[i]
        spill_pO()
        h = hb(i)
        for m in range(8):
            bank, bkey = banks4[m % 4]
            extra = [c["vh"]] if m == 0 else []
            if i == 0 and m == 0:
                extra.append(wdone["wq"])
            if i >= 1 and m == 0:
                extra.append(("GP", ctx[i - 1]["gp_ofin"]))

            def evac_q(t_end, bank, bkey, _m=m):
                skey = "sscr%d" % (_m % 2)
                s_c = ew("S", [("T", t_end)] + RD[skey],
                         lambda e, _mm=_m, _b=bank: e.activation(
                             sscr[:, (_mm % 2) * MC:(_mm % 2 + 1) * MC], _b[:, 0:MC],
                             AF.Sigmoid), inc="S")
                RD[skey] = []
                v_c = ew("V", [("S", s_c)],
                         lambda e, _mm=_m, _b=bank: e.scalar_tensor_tensor(
                             qt_t[:, _mm * MC:(_mm + 1) * MC],
                             sscr[:, (_mm % 2) * MC:(_mm % 2 + 1) * MC],
                             SCALE, _b[:, 0:MC], OP.mult, OP.mult), inc="V")
                RD[skey].append(("V", v_c))
                RD[bkey] = [("V", v_c)]
            proj("wq", 1024, h, m, bank, bkey, extra, evac_q)

        bank, bkey = banks4[0]

        def evac_g1(t_end, bank, bkey):
            v_c = ew("V", [("T", t_end)],
                     lambda e, _b=bank: e.tensor_copy(g1_t[:, :], _b[:, 0:MC]), inc="V")
            RD[bkey] = [("V", v_c)]
        extra = [wdone["wg1"], wdone["wg2"]] if i == 0 else []
        proj("wg1", 128, h, 0, bank, bkey, extra, evac_g1)
        v_g1 = CNT["V"]

        for m in range(8):
            bank, bkey = banks4[(m + 1) % 4]
            extra_g2 = [("V", v_g1)] if m == 0 else []

            def evac_g2(t_end, bank, bkey, _m=m):
                skey = "sscr%d" % (_m % 2)
                s_c = ew("S", [("T", t_end)] + RD[skey],
                         lambda e, _mm=_m, _b=bank: e.activation(
                             sscr[:, (_mm % 2) * MC:(_mm % 2 + 1) * MC], _b[:, 0:MC],
                             AF.Sigmoid), inc="S")
                RD[skey] = []
                v_c = ew("V", [("S", s_c)],
                         lambda e, _mm=_m, _b=bank: e.tensor_mul(
                             sg_t[:, _mm * MC:(_mm + 1) * MC],
                             sscr[:, (_mm % 2) * MC:(_mm % 2 + 1) * MC],
                             _b[:, 0:MC]), inc="V")
                RD[skey].append(("V", v_c))
                RD[bkey] = [("V", v_c)]
            proj("wg2", 1024, g1_t, m, bank, bkey, extra_g2, evac_g2)

    def phB2(i):
        c = ctx[i]
        spill_pO()
        h = hb(i)
        s_expk = {}
        for m in range(8):
            bank, bkey = banks4[m % 4]
            extra = [wdone["wk"]] if i == 0 and m == 0 else []

            def evac_k(t_end, bank, bkey, _m=m):
                s_c = ew("S", [("T", t_end)],
                         lambda e, _mm=_m, _b=bank: e.activation(
                             ufm_t[:, _mm * MC:(_mm + 1) * MC], _b[:, 0:MC], AF.Exp),
                         inc="S")
                s_expk[_m] = s_c
                RD[bkey] = [("S", s_c)]
            proj("wk", 1024, h, m, bank, bkey, extra, evac_k)

        v_qt, v_scan, s_rE = {}, {}, {}

        def emit_scan(cc):
            sl = cc % 2
            w = [("S", s_expk[cc])]
            if cc >= 2:
                w.append(("V", v_qt[cc - 2]))
            if cc == 0 and i >= 1:
                w.append(("T", ctx[i - 1]["t_v2"]))
            if i == 0:
                v_scan[cc] = ew("V", w, lambda e, _c=cc, _s=sl: e.tensor_tensor_scan(
                    E_s[:, _s * MC:(_s + 1) * MC], ufm_t[:, _c * MC:(_c + 1) * MC],
                    ones256[:, :], 0.0, OP.add, OP.mult), inc="V")
            else:
                v_scan[cc] = ew("V", w, lambda e, _c=cc, _s=sl: e.tensor_tensor_scan(
                    E_s[:, _s * MC:(_s + 1) * MC], ufm_t[:, _c * MC:(_c + 1) * MC],
                    ones256[:, :], ecar[:, _c:_c + 1], OP.add, OP.mult), inc="V")

        emit_scan(0)
        emit_scan(1)
        for cc in range(8):
            sl = cc % 2
            w = [("V", v_scan[cc])]
            if cc >= 2:
                w.append(("S", s_rE[cc - 2]))
            ew("S", w, lambda e, _s=sl: e.activation(
                eA[:, _s * MC:(_s + 1) * MC], E_s[:, _s * MC:(_s + 1) * MC], AF.Ln))
            s_rE[cc] = ew("S", [], lambda e, _s=sl: e.activation(
                eR[:, _s * MC:(_s + 1) * MC], eA[:, _s * MC:(_s + 1) * MC], AF.Exp,
                scale=np.float32(-1.0)), inc="S")
            ew("V", [("S", s_rE[cc])], lambda e, _c=cc, _s=sl: e.tensor_mul(
                qf_t[:, _c * MC:(_c + 1) * MC], qt_t[:, _c * MC:(_c + 1) * MC],
                eR[:, _s * MC:(_s + 1) * MC]))
            v_qt[cc] = ew("V", [], lambda e, _c=cc, _s=sl: e.tensor_copy(
                ecar[:, _c:_c + 1], E_s[:, (_s + 1) * MC - 1:(_s + 1) * MC]), inc="V")
            if cc + 2 < 8:
                emit_scan(cc + 2)
        c["v_qt_all"] = v_qt[7]

    def tmproj(wname, h, mu, region_mms, free_w, extra_w):
        for kc in range(8):
            for hh in range(2):
                w = []
                if kc == 0 and hh == 0:
                    w = free_w + extra_w
                out_ap = region_mms(kc, hh)
                ew("T", w, mm(out_ap,
                              h[:, kc * MC + mu * MU:kc * MC + (mu + 1) * MU],
                              wb[wname][:, kc * 1024 + hh * 512:kc * 1024 + (hh + 1) * 512],
                              start=(kc == 0), stop=(kc == 7)),
                   inc=("T" if kc == 7 and hh == 1 else None))
        return CNT["T"]

    def phC(i):
        c = ctx[i]
        h = hb(i)
        v_qt_all = c["v_qt_all"]
        w0 = list(RD["pX"])
        RD["pX"] = []
        extra = [wdone["wv"]] if i == 0 else []
        t_ktm0 = tmproj("wk", h, 0, lambda kc, hh: pX[:, hh * 512:(hh + 1) * 512],
                        w0, extra)
        s_utm0 = ew("S", [("T", t_ktm0)], lambda e: e.activation(
            utm_t[:, 0:1024], pX[:, :], AF.Exp), inc="S")
        wp = list(RD["pPa"]) + list(RD["pPb"])
        RD["pPa"] = []
        RD["pPb"] = []
        t_vtm0 = tmproj("wv", h, 0, lambda kc, hh: pP[hh][:, 0:512], wp, [])
        wv0 = [("T", t_vtm0)]
        if i >= 1:
            wv0.append(("V", ctx[i - 1]["v_final"]))
        ew("S", wv0, lambda e: e.copy(vtm_t[:, 0:512], pPa[:, 0:512]))
        s_vtm0 = ew("S", [], lambda e: e.copy(
            vtm_t[:, 512:1024], pPb[:, 0:512]), inc="S")
        wO = list(RD["pO"]) + list(RD["pO0"]) + list(RD["pO1"])
        RD["pO"] = []
        RD["pO0"] = []
        RD["pO1"] = []
        t_ktm1 = tmproj("wk", h, 1, lambda kc, hh: pO[:, hh * 512:(hh + 1) * 512],
                        wO, [])
        s_utm1 = ew("S", [("T", t_ktm1)], lambda e: e.activation(
            utm_t[:, 1024:2048], pO[:, :], AF.Exp), inc="S")
        t_vtm1 = tmproj("wv", h, 1, lambda kc, hh: pX[:, hh * 512:(hh + 1) * 512],
                        [("S", s_utm0)], [])
        s_vtm1 = ew("S", [("T", t_vtm1)], lambda e: e.copy(
            vtm_t[:, 1024:2048], pX[:, :]), inc="S")

        for hh in range(8):
            w = [("S", s_vtm0), ("V", v_qt_all)] if hh == 0 else []
            pb = pP[hh // 4]
            ew("T", w, mm(pb[:, (hh % 4) * 128:(hh % 4 + 1) * 128],
                          ufm_t[:, hh * MC:hh * MC + MU],
                          qf_t[:, hh * MC:hh * MC + MU],
                          start=True, stop=True),
               inc=("T" if hh == 7 else None))
        t_at0 = CNT["T"]
        for hh in range(8):
            w = [("S", s_utm1)] if hh == 0 else []
            ew("T", w, mm(pO[:, hh * 128:(hh + 1) * 128],
                          ufm_t[:, hh * MC + MU:hh * MC + 2 * MU],
                          qf_t[:, hh * MC + MU:hh * MC + 2 * MU],
                          start=True, stop=True),
               inc=("T" if hh == 7 else None))
        t_at1 = CNT["T"]

        w = [("T", t_at0)] + RD["atm0"] + RD["atm1"]
        RD["atm0"] = []
        RD["atm1"] = []
        if i == 0:
            w.append(("DV", dw_vecs))
        ew("V", w, lambda e: e.tensor_tensor(
            atm_t[:, 0:512].rearrange("p (h t) -> p h t", h=4),
            pPa[:, 0:512].rearrange("p (h t) -> p h t", h=4),
            bass.AP(msk_s, 0, [[128, 128], [0, 4], [1, 128]]), OP.mult))
        v_msk0 = ew("V", [], lambda e: e.tensor_tensor(
            atm_t[:, 512:1024].rearrange("p (h t) -> p h t", h=4),
            pPb[:, 0:512].rearrange("p (h t) -> p h t", h=4),
            bass.AP(msk_s, 0, [[128, 128], [0, 4], [1, 128]]), OP.mult), inc="V")
        v_msk1 = ew("V", [("T", t_at1)], lambda e: e.tensor_tensor(
            atm_t[:, 1024:2048].rearrange("p (h t) -> p h t", h=8),
            pO[:, :].rearrange("p (h t) -> p h t", h=8),
            bass.AP(msk_s, 0, [[128, 128], [0, 8], [1, 128]]), OP.mult), inc="V")
        RD["pO"] = [("V", v_msk1)]

        ps_free = list(RD["pS"])
        RD["pS"] = []
        for hh in range(8):
            w = (ps_free + [("S", s_vtm0), ("S", s_utm0)]) if hh == 0 else []
            ew("T", w, mm(pS[:, hh * 128:(hh + 1) * 128],
                          utm_t[:, hh * 128:(hh + 1) * 128],
                          vtm_t[:, hh * 128:(hh + 1) * 128],
                          start=(i == 0 and hh % 4 == 0), stop=False),
               inc=("T" if hh == 7 else None))

        ops = ([] if i == 0 else [("inter", hh) for hh in range(8)])
        ops += [("intra", hh) for hh in range(8)]
        started = set()
        for idx, (kind, hh) in enumerate(ops):
            bnk = hh // 4
            st = bnk not in started
            started.add(bnk)
            w = []
            if idx == 0:
                w = [("V", v_msk0), ("S", s_vtm1)]
                if i > 0:
                    w.append(ctx[i - 1]["ssb_done"])
            if kind == "inter":
                bld = mm(pX[:, hh * 128:(hh + 1) * 128],
                         ssb_t[:, hh * 128:(hh + 1) * 128],
                         qf_t[:, hh * MC:hh * MC + MU], start=st, stop=False)
            else:
                bld = mm(pX[:, hh * 128:(hh + 1) * 128],
                         vtm_t[:, hh * 128:(hh + 1) * 128],
                         atm_t[:, hh * 128:(hh + 1) * 128],
                         start=st, stop=(idx == len(ops) - 1))
            ew("T", w, bld, inc=("T" if idx == len(ops) - 1 else None))
        t_o0 = CNT["T"]

        s_ssb0 = ew("S", [("T", t_o0)],
                    lambda e: e.copy(ssb_t[:, :], pS[:, :]), inc="S")

        for hh in range(8):
            w = [("S", s_ssb0), ("S", s_vtm1), ("S", s_utm1)] if hh == 0 else []
            ew("T", w, mm(pS[:, hh * 128:(hh + 1) * 128],
                          utm_t[:, 1024 + hh * 128:1024 + (hh + 1) * 128],
                          vtm_t[:, 1024 + hh * 128:1024 + (hh + 1) * 128],
                          start=False,
                          stop=(i == NMC - 1 and hh % 4 == 3)),
               inc=("T" if hh == 7 else None))

        ops = [("inter", hh) for hh in range(8)] + [("intra", hh) for hh in range(8)]
        started = set()
        for idx, (kind, hh) in enumerate(ops):
            pb = pP[hh // 4]
            bnk = hh // 4
            st = bnk not in started
            started.add(bnk)
            w = []
            if idx == 0:
                w = [("V", v_msk1), ("V", v_msk0)]
            if kind == "inter":
                bld = mm(pb[:, (hh % 4) * 128:(hh % 4 + 1) * 128],
                         ssb_t[:, hh * 128:(hh + 1) * 128],
                         qf_t[:, hh * MC + MU:hh * MC + 2 * MU], start=st, stop=False)
            else:
                bld = mm(pb[:, (hh % 4) * 128:(hh % 4 + 1) * 128],
                         vtm_t[:, 1024 + hh * 128:1024 + (hh + 1) * 128],
                         atm_t[:, 1024 + hh * 128:1024 + (hh + 1) * 128],
                         start=st, stop=(idx == len(ops) - 1))
            ew("T", w, bld, inc=("T" if idx == len(ops) - 1 else None))
        t_o1 = CNT["T"]

        s_ssb1 = ew("S", [("T", t_o1)],
                    lambda e: e.copy(ssb_t[:, :], pS[:, :]), inc="S")
        c["ssb_done"] = ("S", s_ssb1)
        RD["pS"] = [c["ssb_done"]]

        woev = [("T", t_o0)]
        if i >= 1:
            woev.append(("GP", ctx[i - 1]["gp_ofin"]))
        v_oev0 = ew("V", woev, lambda e: e.tensor_copy(
            or_t[:, :].rearrange("p (c t) -> p c t", c=8)[:, :, 0:MU],
            pX[:, :].rearrange("p (h t) -> p h t", h=8)), inc="V")
        RD["pX"] = [("V", v_oev0)]
        ew("V", [("T", t_o1)], lambda e: e.tensor_copy(
            or_t[:, :].rearrange("p (c t) -> p c t", c=8)[:, 0:4, MU:2 * MU],
            pPa[:, 0:512].rearrange("p (h t) -> p h t", h=4)))
        v_oev1 = ew("V", [], lambda e: e.tensor_copy(
            or_t[:, :].rearrange("p (c t) -> p c t", c=8)[:, 4:8, MU:2 * MU],
            pPb[:, 0:512].rearrange("p (h t) -> p h t", h=4)), inc="V")
        RD["pPa"] = [("V", v_oev1)]
        RD["pPb"] = [("V", v_oev1)]
        RD["atm0"] = [("T", t_o0)]
        RD["atm1"] = [("T", t_o1)]
        c["t_o1"] = t_o1

    def phD(i):
        c = ctx[i]
        spill_pO()
        xa = xin[i % 2]
        v_osq = ew("V", [], lambda e: e.tensor_mul(qt_t[:, :], or_t[:, :],
                                                   or_t[:, :]), inc="V")
        for kc in range(8):
            w = ([("V", v_osq)] + RD["pPa"]) if kc == 0 else []
            if kc == 0:
                RD["pPa"] = []
            ew("T", w, mm(pPa[:, 0:MC], ones256[:, 0:128],
                          qt_t[:, kc * MC:(kc + 1) * MC],
                          start=(kc == 0), stop=(kc == 7)),
               inc=("T" if kc == 7 else None))
        t_o2 = CNT["T"]
        ew("S", [("T", t_o2)],
           lambda e: e.activation(sA[:, :], pPa[:, 0:MC], AF.Ln,
                                  bias=np.float32(EPS), scale=np.float32(1.0 / D)))
        s_ro = ew("S", [], lambda e: e.activation(sC[:, :], sA[:, :], AF.Exp,
                                                  scale=np.float32(-0.5)), inc="S")
        RD["pPa"] = [("S", s_ro)]
        v_or_all = ("V", CNT["V"])
        for kc in range(8):
            eng = "V" if kc < 5 else "GP"
            ob_ = obuf if eng == "V" else gob
            w = [("S", s_ro)] if kc in (0, 5) else []
            if eng == "GP" and kc == 5:
                w.append(v_or_all)
            ew(eng, w, lambda e, _k=kc, _o=ob_: e.tensor_mul(
                _o[:, :], or_t[:, _k * MC:(_k + 1) * MC],
                sg_t[:, _k * MC:(_k + 1) * MC]))
            ew(eng, [], lambda e, _k=kc, _o=ob_: e.tensor_mul(
                qt_t[:, _k * MC:(_k + 1) * MC], _o[:, :], sC[:, :]),
               inc=(eng if kc in (4, 7) else None))
        v_ofin, gp_ofin = ("V", CNT["V"]), ("GP", CNT["GP"])
        c["gp_ofin"] = CNT["GP"]

        for m in range(8):
            bank, bkey = banks4[m % 4]
            extra = [v_ofin, gp_ofin] if m == 0 else []
            if i == 0 and m == 0:
                extra.append(wdone["wo"])

            def evac_wo(t_end, bank, bkey, _m=m):
                v_c = ew("V", [("T", t_end)],
                         lambda e, _mm=_m, _b=bank, _xa=xa: e.scalar_tensor_tensor(
                             xmid[:, _mm * MC:(_mm + 1) * MC], _b[:, 0:MC],
                             modv[:, M_GA + _mm:M_GA + _mm + 1],
                             _xa[:, _mm * MC:(_mm + 1) * MC], OP.mult, OP.add),
                         inc="V")
                RD[bkey] = [("V", v_c)]
            proj("wo", 1024, qt_t, m, bank, bkey, extra, evac_wo)
        c["v_resid"] = CNT["V"]

    def phE_front(i):
        c = ctx[i]
        if i + 1 < nmc:
            xtra = [("T", ctx[i + 1]["t_statsA"])]
        else:
            xtra = [("T", c["t_o1"])]
        xtra.append(("GP", CNT["GP"]))
        c["t_stats2"] = ln_stats_front(xmid, xtra, pPa[:, 0:MC], pPb[:, 0:MC],
                                       "pPa", "pPb", or_t)

    def phE_back(i):
        c = ctx[i]
        s_r = ln_stats_back(c["t_stats2"], pPa[:, 0:MC], pPb[:, 0:MC], "pPa", "pPb")
        c["vh2"] = ln_apply(xmid, hb(i), s_r, M_CB1, M_SB)

    def phF(i):
        c = ctx[i]
        h = hb(i)
        for kc in range(8):
            eng = "V" if kc < 5 else "GP"
            w = []
            if eng == "GP" and kc == 5:
                w = [c["vh2"]]
            ew(eng, w, lambda e, _k=kc, _h=h: e.tensor_sub(
                dl_ap(_k, 0, 1), _h[:, _k * MC:_k * MC + 1], h2car[:, _k:_k + 1]))
            ew(eng, [], lambda e, _k=kc, _h=h: e.tensor_sub(
                dl_ap(_k, 1, MC), _h[:, _k * MC + 1:(_k + 1) * MC],
                _h[:, _k * MC:(_k + 1) * MC - 1]),
               inc=(eng if kc in (4, 7) else None))
        gp_dl = ("GP", CNT["GP"])
        for kc in range(8):
            w = [gp_dl] if kc == 5 else []
            ew("V", w, lambda e, _k=kc, _h=h: e.scalar_tensor_tensor(
                ufm_t[:, _k * MC:(_k + 1) * MC], dl_ap(_k, 0, MC),
                modv[:, M_MUR + _k:M_MUR + _k + 1], _h[:, _k * MC:(_k + 1) * MC],
                OP.mult, OP.add), inc=("V" if kc == 7 else None))
        c["v_xr"] = ("V", CNT["V"])
        ew("V", [gp_dl], lambda e, _h=h: e.tensor_copy(
            h2car[:, :].rearrange("p (c o) -> p c o", o=1),
            _h[:, :].rearrange("p (c t) -> p c t", c=8)[:, :, MC - 1:MC]))
        for kc in range(8):
            ew("V", [], lambda e, _k=kc, _h=h: e.scalar_tensor_tensor(
                utm_t[:, _k * MC:(_k + 1) * MC], dl_ap(_k, 0, MC),
                modv[:, M_MUK + _k:M_MUK + _k + 1], _h[:, _k * MC:(_k + 1) * MC],
                OP.mult, OP.add), inc=("V" if kc == 7 else None))
        c["v_xk"] = ("V", CNT["V"])

    def phG(i):
        c = ctx[i]
        spill_pO()
        xa = xin[i % 2]
        s_sigr = {}
        for m in range(8):
            bank, bkey = banks4[m % 4]
            extra = [c["v_xr"]] if m == 0 else []
            if i == 0 and m == 0:
                extra.append(wdone["wr"])

            def evac_r(t_end, bank, bkey, _m=m):
                s_c = ew("S", [("T", t_end)],
                         lambda e, _mm=_m, _b=bank: e.activation(
                             vtm_t[:, _mm * MC:(_mm + 1) * MC], _b[:, 0:MC],
                             AF.Sigmoid), inc="S")
                s_sigr[_m] = s_c
                RD[bkey] = [("S", s_c)]
            proj("wr", 1024, ufm_t, m, bank, bkey, extra, evac_r)

        v_kact_m = {}
        for m in range(8):
            bank, bkey = banks4[m % 4]
            extra = [c["v_xk"]] if m == 0 else []
            if i == 0 and m == 0:
                extra.append(wdone["wk2"])

            def evac_k2(t_end, bank, bkey, _m=m):
                sl = _m % 2
                w = [("T", t_end)]
                if _m >= 2:
                    w.append(("V", v_kact_m[_m - 2]))
                s_c = ew("S", w, lambda e, _b=bank, _s=sl: e.activation(
                    krl[:, _s * MC:(_s + 1) * MC], _b[:, 0:MC], AF.Relu), inc="S")
                v_c = ew("V", [("S", s_c)],
                         lambda e, _mm=_m, _b=bank, _s=sl: e.tensor_mul(
                             qf_t[:, _mm * MC:(_mm + 1) * MC], _b[:, 0:MC],
                             krl[:, _s * MC:(_s + 1) * MC]), inc="V")
                v_kact_m[_m] = v_c
                RD[bkey] = [("V", v_c)]
            proj("wk2", 1024, utm_t, m, bank, bkey, extra, evac_k2)
        v_kact = CNT["V"]

        for m in range(8):
            bank, bkey = banks4[m % 4]
            extra = [("V", v_kact), ("S", s_sigr[7])] if m == 0 else []
            if i == 0 and m == 0:
                extra.append(wdone["wv2"])

            def evac_v2(t_end, bank, bkey, _m=m):
                ew("V", [("T", t_end)], lambda e, _mm=_m, _b=bank: e.tensor_mul(
                    f0[:, :], _b[:, 0:MC], vtm_t[:, _mm * MC:(_mm + 1) * MC]))
                v_c = ew("V", [], lambda e, _mm=_m, _xa=xa: e.scalar_tensor_tensor(
                    _xa[:, _mm * MC:(_mm + 1) * MC], f0[:, :],
                    modv[:, M_GB + _mm:M_GB + _mm + 1],
                    xmid[:, _mm * MC:(_mm + 1) * MC], OP.mult, OP.add), inc="V")
                RD[bkey] = [("V", v_c)]
            proj("wv2", 1024, qf_t, m, bank, bkey, extra, evac_v2)
        c["t_v2"] = CNT["T"]
        v_final = CNT["V"]
        c["v_final"] = v_final

        ew("Y", [("V", v_final)],
           lambda e, _mc=i, _xa=xa: e.dma_start(
               out=dr["y"].rearrange("(c p) t -> p c t", p=128)[:, :, _mc * MC:(_mc + 1) * MC],
               in_=_xa[:, :].rearrange("p (c t) -> p c t", c=NCH)),
           inc="DO%d" % (i % 2), inc_by=16)
        if i + 2 < nmc:
            emit_xin(i + 2)

    phA_front(0)
    phA_back(0)
    phB1(0)
    phB2(0)
    phC(0)
    for mc in range(nmc):
        phD(mc)
        if attn_only:
            v_fin = ctx[mc]["v_resid"]
            ew("Y", [("V", v_fin)],
               lambda e, _mc=mc: e.dma_start(
                   out=dr["y"].rearrange("(c p) t -> p c t", p=128)[:, :, _mc * MC:(_mc + 1) * MC],
                   in_=xin[_mc % 2][:, :].rearrange("p (c t) -> p c t", c=NCH)),
               inc="DO%d" % (mc % 2), inc_by=16)
            if mc + 2 < nmc:
                emit_xin(mc + 2)
            if mc + 1 < nmc:
                phA_front(mc + 1)
                phA_back(mc + 1)
                phB1(mc + 1)
                phB2(mc + 1)
                phC(mc + 1)
            continue
        if mc + 1 < nmc:
            phA_front(mc + 1)
        phE_front(mc)
        if mc + 1 < nmc:
            phA_back(mc + 1)
        phE_back(mc)
        if mc + 1 < nmc:
            phB1(mc + 1)
        phF(mc)
        phG(mc)
        if mc + 1 < nmc:
            phB2(mc + 1)
            phC(mc + 1)

    n_do0 = (nmc + 1) // 2
    n_do1 = nmc // 2
    ew("G", [("DO0", 16 * n_do0), ("DO1", 16 * n_do1)], None)

    if dump:
        name2buf = {"h_t": h_t, "qt_t": qt_t, "qf_t": qf_t, "ufm_t": ufm_t,
                    "utm_t": utm_t, "vtm_t": vtm_t, "sg_t": sg_t, "or_t": or_t,
                    "atm_t": atm_t, "ssb_t": ssb_t, "xmid": xmid, "modv": modv,
                    "E_s": E_s, "ecar": ecar, "g1_t": g1_t, "sR": sR, "sM": sM,
                    "xin0": xin[0], "silc": silc, "silc_bf": silc_bf, "mtmp": mtmp,
                    "cv_s": cv_s, "sscr": sscr, "ones256": ones256}
        w = [("T", CNT["T"]), ("V", CNT["V"]), ("S", CNT["S"])]
        for di, nm in enumerate(dump):
            buf = name2buf[nm]
            dd = nc.dram_tensor("dbg_" + nm, list(buf.ap().shape),
                                buf.ap().dtype, kind="ExternalOutput").ap()
            ew("G", (w if di == 0 else []),
               lambda e, _d=dd, _b=buf: e.dma_start(out=_d, in_=_b[:, :]),
               inc="DO0", inc_by=16)
        ew("G", [("DO0", CNT["DO0"])], None)

    with nc.Block() as block:
        @block.tensor
        def _(e):
            for op in ST["T"]:
                op(e)

        @block.vector
        def _(e):
            for op in ST["V"]:
                op(e)

        @block.scalar
        def _(e):
            for op in ST["S"]:
                op(e)

        @block.sync
        def _(e):
            for op in ST["Y"]:
                op(e)

        @block.gpsimd
        def _(e):
            for op in ST["G"]:
                op(e)

    stack.close()
    return nc


_NC_CACHE = None


def kernel(**ins):
    global _NC_CACHE
    if _NC_CACHE is None:
        _NC_CACHE = build_nc()
    nc = _NC_CACHE

    def tr(a):
        return np.ascontiguousarray(np.asarray(a, dtype=np.float32).T)

    def vec8(v):
        return np.ascontiguousarray(np.asarray(v, np.float32).reshape(8, 128).T)

    x = np.asarray(ins["L0_attn_hidden_states"], np.float32)
    shared = {
        "wq": tr(ins["L0_attn_Wq"]),
        "wk": tr(ins["L0_attn_Wk"]),
        "wv": tr(ins["L0_attn_Wv"]),
        "wg1": tr(ins["L0_attn_Wg1"]),
        "wg2": tr(ins["L0_attn_Wg2"]),
        "wo": tr(ins["L0_attn_Wo"]),
        "wr": tr(ins["L0_ffn_W_r"]),
        "wk2": tr(ins["L0_ffn_W_k"]),
        "wv2": tr(ins["L0_ffn_W_v"]),
        "wmod": np.ascontiguousarray(np.asarray(ins["L0_wmod"], np.float32)),
        "gnw": vec8(ins["L0_attn_g_norm_weight"]),
        "mur": vec8(ins["L0_ffn_mu_r"]),
        "muk": vec8(ins["L0_ffn_mu_k"]),
        "msk": np.ascontiguousarray(np.triu(np.ones((128, 128), np.float32))),
    }
    in_maps = []
    for b in range(8):
        m = dict(shared)
        m["x"] = tr(x[b])
        m["cvec"] = vec8(np.asarray(ins["L0_c"], np.float32)[b])
        in_maps.append(m)

    res = run_bass_kernel_spmd(nc, in_maps, core_ids=list(range(8)))
    out = np.stack([np.ascontiguousarray(np.asarray(r["y"], np.float32).T)
                    for r in res.results], axis=0)
    return out


# version 11
# speedup vs baseline: 1.0491x; 1.0491x over previous
import contextlib

import numpy as np

import concourse.bass as bass
import concourse.mybir as mybir
from concourse.bass_utils import run_bass_kernel_spmd

F32 = mybir.dt.float32
BF16 = mybir.dt.bfloat16
AF = mybir.ActivationFunctionType
OP = mybir.AluOpType

T = 2048
D = 1024
NCH = 8
MC = 256
NMC = T // MC
MU = 128
NMU = MC // MU
NMUG = NMC * NMU
H = 8
SCALE = float(128) ** -0.5
EPS = 1e-5

WEIGHT_SPECS = [
    ("wq", 8, 1024, False),
    ("wg1", 8, 128, False),
    ("wg2", 1, 1024, False),
    ("wk", 8, 1024, False),
    ("wv", 8, 1024, False),
    ("wo", 8, 1024, True),
    ("wr", 8, 1024, False),
    ("wk2", 8, 1024, False),
    ("wv2", 8, 1024, False),
]

M_SA, M_CA1, M_GA, M_SB, M_CB1, M_GB, M_MUR, M_MUK = 0, 8, 16, 24, 32, 40, 48, 56


def build_nc(nmc=NMC, dump=(), attn_only=False, cutoff=None, wlimit=None, wmode='both'):
    nc = bass.Bass("TRN2", target_bir_lowering=False, detect_race_conditions=False)
    stack = contextlib.ExitStack()

    dr = {"x": nc.dram_tensor("x", [D, T], F32, kind="ExternalInput").ap()}
    for nm, nch, dout, _ in WEIGHT_SPECS:
        dr[nm] = nc.dram_tensor(nm, [nch * 128, dout], F32, kind="ExternalInput").ap()
    dr["wmod"] = nc.dram_tensor("wmod", [1024, 6144], F32, kind="ExternalInput").ap()
    for nm in ("cvec", "gnw", "mur", "muk"):
        dr[nm] = nc.dram_tensor(nm, [128, 8], F32, kind="ExternalInput").ap()
    dr["msk"] = nc.dram_tensor("msk", [128, 128], F32, kind="ExternalInput").ap()
    dr["y"] = nc.dram_tensor("y", [D, T], F32, kind="ExternalOutput").ap()

    ST = {"T": [], "V": [], "S": [], "Y": [], "G": []}
    CNT = {k: 0 for k in ("T", "V", "S", "GP", "DV", "DW0", "DW1", "DW2", "DW3",
                          "DX0", "DX1", "DO0", "DO1",
                          "Wwq", "Wwg1", "Wwg2", "Wwk", "Wwv", "Wwo",
                          "Wwr", "Wwk2", "Wwv2")}
    SEMS = {k: stack.enter_context(nc.semaphore("sem" + k)) for k in CNT}

    def ew(eng, waits, build, inc=None, inc_by=1):
        if eng == "GP":
            eng = "G"
            if inc == "GP" or inc is None:
                inc = inc
        for s, v in waits:
            assert v <= CNT[s], f"wait on future {s}:{v} > {CNT[s]}"

        def op(e, _w=tuple(waits), _b=build, _i=inc, _n=inc_by):
            for s, v in _w:
                if v > 0:
                    e.wait_ge(SEMS[s], v)
            ins = _b(e) if _b is not None else None
            if _i is not None:
                ins.then_inc(SEMS[_i], _n)
        ST[eng].append(op)
        if inc is not None:
            CNT[inc] += inc_by
            return CNT[inc]
        return None

    def sbuf(name, shape, dt):
        return stack.enter_context(nc.sbuf_tensor(name, shape, dt))

    def psum(name, shape):
        return stack.enter_context(nc.psum_tensor(name, shape, F32))

    xin = [sbuf("xin0", [128, T], F32), sbuf("xin1", [128, T], F32)]
    xmid = sbuf("xmid", [128, T], F32)
    stage_bf = [sbuf("stagebf0", [128, 1024], BF16), sbuf("stagebf1", [128, 1024], BF16)]
    wb = {nm: sbuf(nm + "_b", [128, nch * dout], BF16)
          for nm, nch, dout, _ in WEIGHT_SPECS}

    h_t = sbuf("h_t", [128, T], BF16)
    h_u = sbuf("h_u", [128, T], BF16)
    qt_t = sbuf("qt_t", [128, T], BF16)
    qf_t = sbuf("qf_t", [128, T], BF16)
    ufm_t = sbuf("ufm_t", [128, T], BF16)
    utm_t = sbuf("utm_t", [128, T], BF16)
    vtm_t = sbuf("vtm_t", [128, T], BF16)
    sg_t = sbuf("sg_t", [128, T], BF16)
    or_t = sbuf("or_t", [128, T], BF16)
    atm_t = sbuf("atm_t", [128, T], BF16)
    ssb_t = sbuf("ssb_t", [128, 1024], BF16)
    g1_t = sbuf("g1_t", [128, MC], BF16)
    sscr = sbuf("sscr", [128, 2 * MC], BF16)
    krl = sbuf("krl", [128, 2 * MC], BF16)
    obuf = sbuf("obuf", [128, MC], BF16)
    ones256 = sbuf("ones256", [128, MC], BF16)
    silc_bf = sbuf("silc_bf", [128, 8], BF16)
    h2car = sbuf("h2car", [128, 8], BF16)

    E_s = sbuf("E_s", [128, 2 * MC], F32)
    eA = sbuf("eA", [128, 2 * MC], F32)
    eR = sbuf("eR", [128, 2 * MC], F32)
    sM = sbuf("sM", [128, MC], F32)
    sA = sbuf("sA", [128, MC], F32)
    sB = sbuf("sB", [128, MC], F32)
    sC = sbuf("sC", [128, MC], F32)
    sR = sbuf("sR", [128, MC], F32)
    f0 = sbuf("f0", [128, MC], F32)
    gf0 = sbuf("gf0", [128, MC], F32)
    gf1 = sbuf("gf1", [128, MC], F32)
    gob = sbuf("gob", [128, MC], BF16)
    f1 = sbuf("f1", [128, MC], F32)
    f2 = sbuf("f2", [128, MC], F32)
    modv = sbuf("modv", [128, 64], F32)
    mtmp = sbuf("mtmp", [128, 24], F32)
    silc = sbuf("silc", [128, 8], F32)
    cv_s = sbuf("cv_s", [128, 8], F32)
    gnw_s = sbuf("gnw_s", [128, 8], F32)
    mur_s = sbuf("mur_s", [128, 8], F32)
    muk_s = sbuf("muk_s", [128, 8], F32)
    ecar = sbuf("ecar", [128, 8], F32)
    msk_s = sbuf("msk_s", [128, 128], F32)

    pS = psum("pS", [128, 1024])
    pX = psum("pX", [128, 1024])
    pO = psum("pO", [128, 1024])
    pPa = psum("pPa", [128, 512])
    pPb = psum("pPb", [128, 512])

    RD = {k: [] for k in ("pS", "pX", "pO", "pO0", "pO1", "pPa", "pPb",
                          "sscr0", "sscr1", "E_s", "atm0", "atm1")}

    def mm(out, lhsT, rhs, start, stop):
        return lambda e: e.matmul(out, lhsT, rhs, start=start, stop=stop,
                                  skip_group_check=True)

    ew("V", [], lambda e: e.memset(ones256[:, :], 1.0))
    ew("V", [], lambda e: e.memset(h2car[:, :], 0.0))

    for nm, dst in (("cvec", cv_s), ("gnw", gnw_s), ("mur", mur_s),
                    ("muk", muk_s), ("msk", msk_s)):
        ew("Y", [], lambda e, _d=dst, _n=nm: e.dma_start(out=_d[:, :], in_=dr[_n]),
           inc="DV", inc_by=16)
    dw_vecs = CNT["DV"]

    ew("S", [("DV", dw_vecs)],
       lambda e: e.activation(silc[:, :], cv_s[:, :], AF.Sigmoid), inc="S")
    vme = ew("V", [("S", CNT["S"])],
             lambda e: e.tensor_mul(silc[:, :], silc[:, :], cv_s[:, :]), inc="V")
    ew("V", [("V", vme)], lambda e: e.tensor_copy(silc_bf[:, :], silc[:, :]))
    ew("V", [], lambda e: e.tensor_scalar_add(modv[:, M_MUR:M_MUR + 8], mur_s[:, :], -1.0))
    ew("V", [], lambda e: e.tensor_scalar_add(modv[:, M_MUK:M_MUK + 8], muk_s[:, :], -1.0),
       inc="V")
    v_silc = CNT["V"]

    def emit_xin(mc, extra=()):
        w = list(extra)
        if mc >= 2:
            w = [("DO%d" % (mc % 2), 16 * ((mc - 2) // 2 + 1))]
        ew("Y", w,
           lambda e, _mc=mc: e.dma_start(
               out=xin[_mc % 2][:, :].rearrange("p (c t) -> p c t", c=NCH),
               in_=dr["x"].rearrange("(c p) t -> p c t", p=128)[:, :, _mc * MC:(_mc + 1) * MC]),
           inc="DX%d" % (mc % 2), inc_by=16)

    emit_xin(0)

    wm_mm, wm_cast = {}, {}
    stg_buf = {0: xmid, 1: xin[1]}
    for g in range(6):
        for kc in range(8):
            i = g * 8 + kc
            q = i % 2
            eng = "Y" if q == 0 else "S"
            slot = (i // 2) % 2
            xh = stg_buf[q][:, slot * 1024:(slot + 1) * 1024]
            w = [("V", wm_cast[i - 4])] if i >= 4 else []
            dwk = "DW%d" % (q * 2 + slot)
            ew(eng, w,
               lambda e, _g=g, _kc=kc, _x=xh: e.dma_start(
                   out=_x,
                   in_=dr["wmod"][_kc * 128:(_kc + 1) * 128, _g * 1024:(_g + 1) * 1024]),
               inc=dwk, inc_by=16)
            w = [(dwk, CNT[dwk])] + ([("T", wm_mm[i - 2])] if i >= 2 else [])
            wm_cast[i] = ew("V", w,
                            lambda e, _i=i, _x=xh: e.tensor_copy(
                                stage_bf[_i % 2][:, :], _x), inc="V")
            for j in range(8):
                w = [("V", wm_cast[i])] if j == 0 else []
                if i == 0 and j == 0:
                    w.append(("V", v_silc))
                col = kc * 48 + g * 8 + j
                ew("T", w,
                   mm(pX[:, col:col + 1],
                      stage_bf[i % 2][:, j * 128:(j + 1) * 128],
                      silc_bf[:, kc:kc + 1],
                      start=True, stop=True),
                   inc=("T" if j == 7 else None))
            wm_mm[i] = CNT["T"]
    t_mod = CNT["T"]

    def vspacer():
        ew("V", [], lambda e: e.memset(f1[:, :], 0.0))
        ew("V", [], lambda e: e.memset(f2[:, :], 0.0))
    ew("V", [("T", t_mod)], lambda e: e.memset(f1[:, :], 0.0))
    for _ in range(4):
        vspacer()
    vp = ew("V", [], lambda e: e.tensor_copy(
        f0[:, 0:48], pX[:, 0:48]), inc="V")
    for kc in range(1, 8):
        vspacer()
        vp = ew("V", [("V", vp)], lambda e, _k=kc: e.tensor_add(
            f0[:, 0:48], f0[:, 0:48], pX[:, _k * 48:(_k + 1) * 48]), inc="V")
    for _ in range(4):
        vspacer()
    ew("V", [("V", vp)], lambda e: e.tensor_copy(modv[:, M_SA:M_SA + 8], f0[:, 0:8]))
    ew("V", [], lambda e: e.tensor_scalar_add(modv[:, M_CA1:M_CA1 + 8], f0[:, 8:16], 1.0))
    ew("V", [], lambda e: e.tensor_copy(modv[:, M_SB:M_SB + 8], f0[:, 24:32]))
    ew("V", [], lambda e: e.tensor_scalar_add(modv[:, M_CB1:M_CB1 + 8], f0[:, 32:40], 1.0))
    v_f0 = ew("V", [], lambda e: e.tensor_copy(mtmp[:, 16:24], f0[:, 16:24]), inc="V")
    for src_, dst, mo in ((40, M_GB, 8), (16, M_GA, 0)):
        ew("S", [("V", v_f0)], lambda e, _s=src_, _m=mo: e.activation(
            mtmp[:, _m:_m + 8], f0[:, _s:_s + 8], AF.Sign))
        s_sg = ew("S", [], lambda e, _s=src_, _d=dst: e.activation(
            modv[:, _d:_d + 8], f0[:, _s:_s + 8], AF.Abs), inc="S")
        v_am = ew("V", [("S", s_sg)], lambda e, _d=dst: e.tensor_scalar_max(
            modv[:, _d:_d + 8], modv[:, _d:_d + 8], 0.25), inc="V")
        ew("V", [("V", v_am)], lambda e, _d=dst, _m=mo: e.tensor_mul(
            modv[:, _d:_d + 8], modv[:, _d:_d + 8], mtmp[:, _m:_m + 8]), inc="V")
    ew("V", [], lambda e: e.tensor_copy(mtmp[:, 16:24], f0[:, 40:48]), inc="V")
    RD["pX"] = [("V", CNT["V"])]

    def early_exit():
        name2buf = {"modv": modv, "mtmp": mtmp, "silc": silc, "silc_bf": silc_bf,
                    "cv_s": cv_s, "sM": sM, "sR": sR, "xin0": xin[0],
                    "stagebf0": stage_bf[0], "stagebf1": stage_bf[1],
                    "ones256": ones256, "E_s": E_s, "ecar": ecar, "g1_t": g1_t,
                    "h_t": h_t, "qt_t": qt_t, "qf_t": qf_t, "ufm_t": ufm_t,
                    "utm_t": utm_t, "vtm_t": vtm_t, "sg_t": sg_t, "or_t": or_t,
                    "atm_t": atm_t, "ssb_t": ssb_t, "xmid": xmid}
        w = [("T", CNT["T"]), ("V", CNT["V"]), ("S", CNT["S"])]
        dcnt = 0
        for di, nm in enumerate(dump):
            buf = name2buf[nm]
            dd = nc.dram_tensor("dbg_" + nm, list(buf.ap().shape),
                                buf.ap().dtype, kind="ExternalOutput").ap()
            ew("G", (w if di == 0 else []),
               lambda e, _d=dd, _b=buf: e.dma_start(out=_d, in_=_b[:, :]),
               inc="DO0", inc_by=16)
            dcnt += 1
        ew("G", [], lambda e: e.dma_start(
            out=dr["y"].rearrange("(c p) t -> p c t", p=128)[:, :, 0:MC],
            in_=xin[0][:, 0:NCH * MC].rearrange("p (c t) -> p c t", c=NCH)),
           inc="DO0", inc_by=16)
        ew("G", [("DO0", CNT["DO0"])], None)
        with nc.Block() as block:
            @block.tensor
            def _(e):
                for op in ST["T"]:
                    op(e)

            @block.vector
            def _(e):
                for op in ST["V"]:
                    op(e)

            @block.scalar
            def _(e):
                for op in ST["S"]:
                    op(e)

            @block.sync
            def _(e):
                for op in ST["Y"]:
                    op(e)

            @block.gpsimd
            def _(e):
                for op in ST["G"]:
                    op(e)
        stack.close()
        return nc

    if cutoff == "mod":
        return early_exit()

    wdone = {}
    for nm, nch, dout, fold in (WEIGHT_SPECS if wlimit is None else WEIGHT_SPECS[:wlimit]):
        wk_sem = "W" + nm
        for kc in range(nch):
            ew("G", [],
               lambda e, _nm=nm, _kc=kc, _d=dout: e.dma_start(
                   out=wb[_nm][:, _kc * _d:(_kc + 1) * _d],
                   in_=dr[_nm][_kc * 128:(_kc + 1) * 128, :]),
               inc=wk_sem, inc_by=16)
        if fold:
            for kc in range(nch):
                ew("V", ([(wk_sem, 16 * nch)] if kc == 0 else []),
                   lambda e, _nm=nm, _kc=kc, _d=dout: e.tensor_scalar(
                       wb[_nm][:, _kc * _d:(_kc + 1) * _d],
                       wb[_nm][:, _kc * _d:(_kc + 1) * _d],
                       gnw_s[:, _kc:_kc + 1], None, OP.mult),
                   inc=("V" if kc == nch - 1 else None))
            wdone[nm] = ("V", CNT["V"])
        else:
            wdone[nm] = (wk_sem, 16 * nch)

    if cutoff == "weights":
        return early_exit()

    o_dma_cnt = 0
    v_final = 0

    if nmc > 1:
        emit_xin(1, extra=[("T", t_mod)])

    def proj(wname, dout, rhs_t, m, bank, bkey, extra_w, evac):
        nch = next(s[1] for s in WEIGHT_SPECS if s[0] == wname)
        for kc in range(nch):
            w = []
            if kc == 0:
                w = list(RD[bkey]) + extra_w
                RD[bkey] = []
            ew("T", w,
               mm(bank[:, 0:MC],
                  wb[wname][:, kc * dout + m * 128:kc * dout + (m + 1) * 128],
                  rhs_t[:, kc * MC:(kc + 1) * MC],
                  start=(kc == 0), stop=(kc == nch - 1)),
               inc=("T" if kc == nch - 1 else None))
        t_end = CNT["T"]
        evac(t_end, bank, bkey)
        return t_end

    ctx = {i: {} for i in range(nmc)}
    banks4 = [(pPa, "pPa"), (pPb, "pPb"),
              (pO[:, 0:512], "pO0"), (pO[:, 512:1024], "pO1")]
    pP = (pPa, pPb)

    def hb(i):
        return h_t if i % 2 == 0 else h_u

    def spill_pO():
        if RD["pO"]:
            RD["pO0"] = RD["pO0"] + RD["pO"]
            RD["pO1"] = RD["pO1"] + RD["pO"]
            RD["pO"] = []

    def dl_ap(c, lo, hi):
        return stage_bf[c // 4][:, (c % 4) * MC + lo:(c % 4) * MC + hi]

    def ln_stats_front(xsrc, xtra, pa, pb, ka, kb, sq):
        ew("V", xtra, lambda e, _x=xsrc: e.tensor_copy(atm_t[:, :], _x[:, :]))
        ew("V", [], lambda e, _q=sq: e.tensor_mul(_q[:, :], atm_t[:, :], atm_t[:, :]),
           inc="V")
        v_c = CNT["V"]
        for kc in range(8):
            w = ([("V", v_c)] + RD[ka]) if kc == 0 else []
            if kc == 0:
                RD[ka] = []
            ew("T", w, mm(pa, ones256[:, 0:128],
                          atm_t[:, kc * MC:(kc + 1) * MC],
                          start=(kc == 0), stop=(kc == 7)))
        for kc in range(8):
            w = list(RD[kb]) if kc == 0 else []
            if kc == 0:
                RD[kb] = []
            ew("T", w, mm(pb, ones256[:, 0:128],
                          sq[:, kc * MC:(kc + 1) * MC],
                          start=(kc == 0), stop=(kc == 7)),
               inc=("T" if kc == 7 else None))
        return CNT["T"]

    def ln_stats_back(t_st, pa, pb, ka, kb):
        ew("V", [("T", t_st), ("GP", CNT["GP"])],
           lambda e, _p=pa: e.tensor_scalar_mul(sM[:, :], _p, 1.0 / D))
        ew("V", [], lambda e, _p=pb: e.tensor_scalar_mul(sA[:, :], _p, 1.0 / D))
        ew("V", [], lambda e: e.tensor_mul(sB[:, :], sM[:, :], sM[:, :]))
        ew("V", [], lambda e: e.tensor_sub(sC[:, :], sA[:, :], sB[:, :]), inc="V")
        v_st = CNT["V"]
        RD[ka] = [("V", v_st)]
        RD[kb] = [("V", v_st)]
        ew("S", [("V", v_st)], lambda e: e.activation(sA[:, :], sC[:, :], AF.Ln,
                                                      bias=np.float32(EPS)))
        s_r = ew("S", [], lambda e: e.activation(sR[:, :], sA[:, :], AF.Exp,
                                                 scale=np.float32(-0.5)), inc="S")
        return s_r

    def ln_apply(xsrc, hdst, s_rstd, scl_off, add_off):
        gsub = {}
        for kc in (5, 6, 7):
            dst = {5: gf0, 6: gf1, 7: f2}[kc]
            w = [("S", s_rstd)] if kc == 5 else []
            gsub[kc] = ew("GP", w, lambda e, _k=kc, _x=xsrc, _d=dst: e.tensor_sub(
                _d[:, :], _x[:, _k * MC:(_k + 1) * MC], sM[:, :]), inc="GP")
        for kc in range(8):
            if kc < 5:
                w = [("S", s_rstd)] if kc == 0 else []
                ew("V", w, lambda e, _k=kc, _x=xsrc: e.tensor_sub(
                    f0[:, :], _x[:, _k * MC:(_k + 1) * MC], sM[:, :]))
                src_, w2 = f0, []
            else:
                src_ = {5: gf0, 6: gf1, 7: f2}[kc]
                w2 = [("GP", gsub[kc])]
            ew("V", w2, lambda e, _k=kc, _s=scl_off, _sc=src_: e.scalar_tensor_tensor(
                f1[:, :], _sc[:, :], modv[:, _s + _k:_s + _k + 1], sR[:, :],
                OP.mult, OP.mult))
            ew("V", [], lambda e, _k=kc, _a=add_off, _h=hdst: e.tensor_scalar(
                _h[:, _k * MC:(_k + 1) * MC], f1[:, :],
                modv[:, _a + _k:_a + _k + 1], None, OP.add),
               inc=("V" if kc == 7 else None))
        return ("V", CNT["V"])

    def phA_front(i):
        c = ctx[i]
        xa = xin[i % 2]
        xtra = [("DX%d" % (i % 2), 16 * (i // 2 + 1))]
        if i >= 1:
            xtra += list(RD["pX"])
            RD["pX"] = []
        c["t_statsA"] = ln_stats_front(xa, xtra, pX[:, 0:MC], pX[:, 512:512 + MC],
                                       "pX", "pX", qt_t)

    def phA_back(i):
        c = ctx[i]
        xa = xin[i % 2]
        s_r = ln_stats_back(c["t_statsA"], pX[:, 0:MC], pX[:, 512:512 + MC],
                            "pX", "pX")
        c["vh"] = ln_apply(xa, hb(i), s_r, M_CA1, M_SA)

    def phB1(i):
        c = ctx[i]
        spill_pO()
        h = hb(i)
        for m in range(8):
            bank, bkey = banks4[m % 4]
            extra = [c["vh"]] if m == 0 else []
            if i == 0 and m == 0:
                extra.append(wdone["wq"])
            if i >= 1 and m == 0:
                extra.append(("GP", ctx[i - 1]["gp_ofin"]))

            def evac_q(t_end, bank, bkey, _m=m):
                skey = "sscr%d" % (_m % 2)
                s_c = ew("S", [("T", t_end)] + RD[skey],
                         lambda e, _mm=_m, _b=bank: e.activation(
                             sscr[:, (_mm % 2) * MC:(_mm % 2 + 1) * MC], _b[:, 0:MC],
                             AF.Sigmoid), inc="S")
                RD[skey] = []
                v_c = ew("V", [("S", s_c)],
                         lambda e, _mm=_m, _b=bank: e.scalar_tensor_tensor(
                             qt_t[:, _mm * MC:(_mm + 1) * MC],
                             sscr[:, (_mm % 2) * MC:(_mm % 2 + 1) * MC],
                             SCALE, _b[:, 0:MC], OP.mult, OP.mult), inc="V")
                RD[skey].append(("V", v_c))
                RD[bkey] = [("V", v_c)]
            proj("wq", 1024, h, m, bank, bkey, extra, evac_q)

        bank, bkey = banks4[0]

        def evac_g1(t_end, bank, bkey):
            v_c = ew("V", [("T", t_end)],
                     lambda e, _b=bank: e.tensor_copy(g1_t[:, :], _b[:, 0:MC]), inc="V")
            RD[bkey] = [("V", v_c)]
        extra = [wdone["wg1"], wdone["wg2"]] if i == 0 else []
        proj("wg1", 128, h, 0, bank, bkey, extra, evac_g1)
        v_g1 = CNT["V"]

        for m in range(8):
            bank, bkey = banks4[(m + 1) % 4]
            extra_g2 = [("V", v_g1)] if m == 0 else []

            def evac_g2(t_end, bank, bkey, _m=m):
                skey = "sscr%d" % (_m % 2)
                s_c = ew("S", [("T", t_end)] + RD[skey],
                         lambda e, _mm=_m, _b=bank: e.activation(
                             sscr[:, (_mm % 2) * MC:(_mm % 2 + 1) * MC], _b[:, 0:MC],
                             AF.Sigmoid), inc="S")
                RD[skey] = []
                v_c = ew("V", [("S", s_c)],
                         lambda e, _mm=_m, _b=bank: e.tensor_mul(
                             sg_t[:, _mm * MC:(_mm + 1) * MC],
                             sscr[:, (_mm % 2) * MC:(_mm % 2 + 1) * MC],
                             _b[:, 0:MC]), inc="V")
                RD[skey].append(("V", v_c))
                RD[bkey] = [("V", v_c)]
            proj("wg2", 1024, g1_t, m, bank, bkey, extra_g2, evac_g2)

    def phB2(i):
        c = ctx[i]
        spill_pO()
        h = hb(i)
        s_expk = {}
        for m in range(8):
            bank, bkey = banks4[m % 4]
            extra = [wdone["wk"]] if i == 0 and m == 0 else []

            def evac_k(t_end, bank, bkey, _m=m):
                s_c = ew("S", [("T", t_end)],
                         lambda e, _mm=_m, _b=bank: e.activation(
                             ufm_t[:, _mm * MC:(_mm + 1) * MC], _b[:, 0:MC], AF.Exp),
                         inc="S")
                s_expk[_m] = s_c
                RD[bkey] = [("S", s_c)]
            proj("wk", 1024, h, m, bank, bkey, extra, evac_k)

        v_qt, v_scan, s_rE = {}, {}, {}

        def emit_scan(cc):
            sl = cc % 2
            w = [("S", s_expk[cc])]
            if cc >= 2:
                w.append(("V", v_qt[cc - 2]))
            if cc == 0 and i >= 1:
                w.append(("T", ctx[i - 1]["t_v2"]))
            if i == 0:
                v_scan[cc] = ew("V", w, lambda e, _c=cc, _s=sl: e.tensor_tensor_scan(
                    E_s[:, _s * MC:(_s + 1) * MC], ufm_t[:, _c * MC:(_c + 1) * MC],
                    ones256[:, :], 0.0, OP.add, OP.mult), inc="V")
            else:
                v_scan[cc] = ew("V", w, lambda e, _c=cc, _s=sl: e.tensor_tensor_scan(
                    E_s[:, _s * MC:(_s + 1) * MC], ufm_t[:, _c * MC:(_c + 1) * MC],
                    ones256[:, :], ecar[:, _c:_c + 1], OP.add, OP.mult), inc="V")

        emit_scan(0)
        emit_scan(1)
        for cc in range(8):
            sl = cc % 2
            w = [("V", v_scan[cc])]
            if cc >= 2:
                w.append(("S", s_rE[cc - 2]))
            ew("S", w, lambda e, _s=sl: e.activation(
                eA[:, _s * MC:(_s + 1) * MC], E_s[:, _s * MC:(_s + 1) * MC], AF.Ln))
            s_rE[cc] = ew("S", [], lambda e, _s=sl: e.activation(
                eR[:, _s * MC:(_s + 1) * MC], eA[:, _s * MC:(_s + 1) * MC], AF.Exp,
                scale=np.float32(-1.0)), inc="S")
            ew("V", [("S", s_rE[cc])], lambda e, _c=cc, _s=sl: e.tensor_mul(
                qf_t[:, _c * MC:(_c + 1) * MC], qt_t[:, _c * MC:(_c + 1) * MC],
                eR[:, _s * MC:(_s + 1) * MC]))
            v_qt[cc] = ew("V", [], lambda e, _c=cc, _s=sl: e.tensor_copy(
                ecar[:, _c:_c + 1], E_s[:, (_s + 1) * MC - 1:(_s + 1) * MC]), inc="V")
            if cc + 2 < 8:
                emit_scan(cc + 2)
        c["v_qt_all"] = v_qt[7]

    def tmproj(wname, h, mu, region_mms, free_w, extra_w):
        for kc in range(8):
            for hh in range(2):
                w = []
                if kc == 0 and hh == 0:
                    w = free_w + extra_w
                out_ap = region_mms(kc, hh)
                ew("T", w, mm(out_ap,
                              h[:, kc * MC + mu * MU:kc * MC + (mu + 1) * MU],
                              wb[wname][:, kc * 1024 + hh * 512:kc * 1024 + (hh + 1) * 512],
                              start=(kc == 0), stop=(kc == 7)),
                   inc=("T" if kc == 7 and hh == 1 else None))
        return CNT["T"]

    def phC(i):
        c = ctx[i]
        h = hb(i)
        v_qt_all = c["v_qt_all"]
        w0 = list(RD["pX"])
        RD["pX"] = []
        extra = [wdone["wv"]] if i == 0 else []
        t_ktm0 = tmproj("wk", h, 0, lambda kc, hh: pX[:, hh * 512:(hh + 1) * 512],
                        w0, extra)
        s_utm0 = ew("S", [("T", t_ktm0)], lambda e: e.activation(
            utm_t[:, 0:1024], pX[:, :], AF.Exp), inc="S")
        wp = list(RD["pPa"]) + list(RD["pPb"])
        RD["pPa"] = []
        RD["pPb"] = []
        t_vtm0 = tmproj("wv", h, 0, lambda kc, hh: pP[hh][:, 0:512], wp, [])
        wv0 = [("T", t_vtm0)]
        if i >= 1:
            wv0.append(("V", ctx[i - 1]["v_final"]))
        ew("S", wv0, lambda e: e.copy(vtm_t[:, 0:512], pPa[:, 0:512]))
        s_vtm0 = ew("S", [], lambda e: e.copy(
            vtm_t[:, 512:1024], pPb[:, 0:512]), inc="S")
        wO = list(RD["pO"]) + list(RD["pO0"]) + list(RD["pO1"])
        RD["pO"] = []
        RD["pO0"] = []
        RD["pO1"] = []
        t_ktm1 = tmproj("wk", h, 1, lambda kc, hh: pO[:, hh * 512:(hh + 1) * 512],
                        wO, [])
        s_utm1 = ew("S", [("T", t_ktm1)], lambda e: e.activation(
            utm_t[:, 1024:2048], pO[:, :], AF.Exp), inc="S")
        t_vtm1 = tmproj("wv", h, 1, lambda kc, hh: pX[:, hh * 512:(hh + 1) * 512],
                        [("S", s_utm0)], [])
        s_vtm1 = ew("S", [("T", t_vtm1)], lambda e: e.copy(
            vtm_t[:, 1024:2048], pX[:, :]), inc="S")

        for hh in range(8):
            w = [("S", s_vtm0), ("V", v_qt_all)] if hh == 0 else []
            pb = pP[hh // 4]
            ew("T", w, mm(pb[:, (hh % 4) * 128:(hh % 4 + 1) * 128],
                          ufm_t[:, hh * MC:hh * MC + MU],
                          qf_t[:, hh * MC:hh * MC + MU],
                          start=True, stop=True),
               inc=("T" if hh == 7 else None))
        t_at0 = CNT["T"]
        for hh in range(8):
            w = [("S", s_utm1)] if hh == 0 else []
            ew("T", w, mm(pO[:, hh * 128:(hh + 1) * 128],
                          ufm_t[:, hh * MC + MU:hh * MC + 2 * MU],
                          qf_t[:, hh * MC + MU:hh * MC + 2 * MU],
                          start=True, stop=True),
               inc=("T" if hh == 7 else None))
        t_at1 = CNT["T"]

        w = [("T", t_at0)] + RD["atm0"] + RD["atm1"]
        RD["atm0"] = []
        RD["atm1"] = []
        if i == 0:
            w.append(("DV", dw_vecs))
        ew("V", w, lambda e: e.tensor_tensor(
            atm_t[:, 0:512].rearrange("p (h t) -> p h t", h=4),
            pPa[:, 0:512].rearrange("p (h t) -> p h t", h=4),
            bass.AP(msk_s, 0, [[128, 128], [0, 4], [1, 128]]), OP.mult))
        v_msk0 = ew("V", [], lambda e: e.tensor_tensor(
            atm_t[:, 512:1024].rearrange("p (h t) -> p h t", h=4),
            pPb[:, 0:512].rearrange("p (h t) -> p h t", h=4),
            bass.AP(msk_s, 0, [[128, 128], [0, 4], [1, 128]]), OP.mult), inc="V")
        v_msk1 = ew("V", [("T", t_at1)], lambda e: e.tensor_tensor(
            atm_t[:, 1024:2048].rearrange("p (h t) -> p h t", h=8),
            pO[:, :].rearrange("p (h t) -> p h t", h=8),
            bass.AP(msk_s, 0, [[128, 128], [0, 8], [1, 128]]), OP.mult), inc="V")
        RD["pO"] = [("V", v_msk1)]

        ps_free = list(RD["pS"])
        RD["pS"] = []
        for hh in range(8):
            w = (ps_free + [("S", s_vtm0), ("S", s_utm0)]) if hh == 0 else []
            ew("T", w, mm(pS[:, hh * 128:(hh + 1) * 128],
                          utm_t[:, hh * 128:(hh + 1) * 128],
                          vtm_t[:, hh * 128:(hh + 1) * 128],
                          start=(i == 0 and hh % 4 == 0), stop=False),
               inc=("T" if hh == 7 else None))

        ops = ([] if i == 0 else [("inter", hh) for hh in range(8)])
        ops += [("intra", hh) for hh in range(8)]
        started = set()
        for idx, (kind, hh) in enumerate(ops):
            bnk = hh // 4
            st = bnk not in started
            started.add(bnk)
            w = []
            if idx == 0:
                w = [("V", v_msk0), ("S", s_vtm1)]
                if i > 0:
                    w.append(ctx[i - 1]["ssb_done"])
            if kind == "inter":
                bld = mm(pX[:, hh * 128:(hh + 1) * 128],
                         ssb_t[:, hh * 128:(hh + 1) * 128],
                         qf_t[:, hh * MC:hh * MC + MU], start=st, stop=False)
            else:
                bld = mm(pX[:, hh * 128:(hh + 1) * 128],
                         vtm_t[:, hh * 128:(hh + 1) * 128],
                         atm_t[:, hh * 128:(hh + 1) * 128],
                         start=st, stop=(idx == len(ops) - 1))
            ew("T", w, bld, inc=("T" if idx == len(ops) - 1 else None))
        t_o0 = CNT["T"]

        s_ssb0 = ew("S", [("T", t_o0)],
                    lambda e: e.copy(ssb_t[:, :], pS[:, :]), inc="S")

        for hh in range(8):
            w = [("S", s_ssb0), ("S", s_vtm1), ("S", s_utm1)] if hh == 0 else []
            ew("T", w, mm(pS[:, hh * 128:(hh + 1) * 128],
                          utm_t[:, 1024 + hh * 128:1024 + (hh + 1) * 128],
                          vtm_t[:, 1024 + hh * 128:1024 + (hh + 1) * 128],
                          start=False,
                          stop=(i == NMC - 1 and hh % 4 == 3)),
               inc=("T" if hh == 7 else None))

        ops = [("inter", hh) for hh in range(8)] + [("intra", hh) for hh in range(8)]
        started = set()
        for idx, (kind, hh) in enumerate(ops):
            pb = pP[hh // 4]
            bnk = hh // 4
            st = bnk not in started
            started.add(bnk)
            w = []
            if idx == 0:
                w = [("V", v_msk1), ("V", v_msk0)]
            if kind == "inter":
                bld = mm(pb[:, (hh % 4) * 128:(hh % 4 + 1) * 128],
                         ssb_t[:, hh * 128:(hh + 1) * 128],
                         qf_t[:, hh * MC + MU:hh * MC + 2 * MU], start=st, stop=False)
            else:
                bld = mm(pb[:, (hh % 4) * 128:(hh % 4 + 1) * 128],
                         vtm_t[:, 1024 + hh * 128:1024 + (hh + 1) * 128],
                         atm_t[:, 1024 + hh * 128:1024 + (hh + 1) * 128],
                         start=st, stop=(idx == len(ops) - 1))
            ew("T", w, bld, inc=("T" if idx == len(ops) - 1 else None))
        t_o1 = CNT["T"]

        s_ssb1 = ew("S", [("T", t_o1)],
                    lambda e: e.copy(ssb_t[:, :], pS[:, :]), inc="S")
        c["ssb_done"] = ("S", s_ssb1)
        RD["pS"] = [c["ssb_done"]]

        woev = [("T", t_o0)]
        if i >= 1:
            woev.append(("GP", ctx[i - 1]["gp_ofin"]))
        v_oev0 = ew("V", woev, lambda e: e.tensor_copy(
            or_t[:, :].rearrange("p (c t) -> p c t", c=8)[:, :, 0:MU],
            pX[:, :].rearrange("p (h t) -> p h t", h=8)), inc="V")
        RD["pX"] = [("V", v_oev0)]
        ew("V", [("T", t_o1)], lambda e: e.tensor_copy(
            or_t[:, :].rearrange("p (c t) -> p c t", c=8)[:, 0:4, MU:2 * MU],
            pPa[:, 0:512].rearrange("p (h t) -> p h t", h=4)))
        v_oev1 = ew("V", [], lambda e: e.tensor_copy(
            or_t[:, :].rearrange("p (c t) -> p c t", c=8)[:, 4:8, MU:2 * MU],
            pPb[:, 0:512].rearrange("p (h t) -> p h t", h=4)), inc="V")
        RD["pPa"] = [("V", v_oev1)]
        RD["pPb"] = [("V", v_oev1)]
        RD["atm0"] = [("T", t_o0)]
        RD["atm1"] = [("T", t_o1)]
        c["t_o1"] = t_o1

    def phD(i):
        c = ctx[i]
        spill_pO()
        xa = xin[i % 2]
        v_osq = ew("V", [], lambda e: e.tensor_mul(qt_t[:, :], or_t[:, :],
                                                   or_t[:, :]), inc="V")
        for kc in range(8):
            w = ([("V", v_osq)] + RD["pPa"]) if kc == 0 else []
            if kc == 0:
                RD["pPa"] = []
            ew("T", w, mm(pPa[:, 0:MC], ones256[:, 0:128],
                          qt_t[:, kc * MC:(kc + 1) * MC],
                          start=(kc == 0), stop=(kc == 7)),
               inc=("T" if kc == 7 else None))
        t_o2 = CNT["T"]
        ew("S", [("T", t_o2)],
           lambda e: e.activation(sA[:, :], pPa[:, 0:MC], AF.Ln,
                                  bias=np.float32(EPS), scale=np.float32(1.0 / D)))
        s_ro = ew("S", [], lambda e: e.activation(sC[:, :], sA[:, :], AF.Exp,
                                                  scale=np.float32(-0.5)), inc="S")
        RD["pPa"] = [("S", s_ro)]
        v_or_all = ("V", CNT["V"])
        for kc in range(8):
            eng = "V" if kc < 5 else "GP"
            ob_ = obuf if eng == "V" else gob
            w = [("S", s_ro)] if kc in (0, 5) else []
            if eng == "GP" and kc == 5:
                w.append(v_or_all)
            ew(eng, w, lambda e, _k=kc, _o=ob_: e.tensor_mul(
                _o[:, :], or_t[:, _k * MC:(_k + 1) * MC],
                sg_t[:, _k * MC:(_k + 1) * MC]))
            ew(eng, [], lambda e, _k=kc, _o=ob_: e.tensor_mul(
                qt_t[:, _k * MC:(_k + 1) * MC], _o[:, :], sC[:, :]),
               inc=(eng if kc in (4, 7) else None))
        v_ofin, gp_ofin = ("V", CNT["V"]), ("GP", CNT["GP"])
        c["gp_ofin"] = CNT["GP"]

        for m in range(8):
            bank, bkey = banks4[m % 4]
            extra = [v_ofin, gp_ofin] if m == 0 else []
            if i == 0 and m == 0:
                extra.append(wdone["wo"])

            def evac_wo(t_end, bank, bkey, _m=m):
                v_c = ew("V", [("T", t_end)],
                         lambda e, _mm=_m, _b=bank, _xa=xa: e.scalar_tensor_tensor(
                             xmid[:, _mm * MC:(_mm + 1) * MC], _b[:, 0:MC],
                             modv[:, M_GA + _mm:M_GA + _mm + 1],
                             _xa[:, _mm * MC:(_mm + 1) * MC], OP.mult, OP.add),
                         inc="V")
                RD[bkey] = [("V", v_c)]
            proj("wo", 1024, qt_t, m, bank, bkey, extra, evac_wo)
        c["v_resid"] = CNT["V"]

    def phE_front(i):
        c = ctx[i]
        if i + 1 < nmc:
            xtra = [("T", ctx[i + 1]["t_statsA"])]
        else:
            xtra = [("T", c["t_o1"])]
        xtra.append(("GP", CNT["GP"]))
        c["t_stats2"] = ln_stats_front(xmid, xtra, pPa[:, 0:MC], pPb[:, 0:MC],
                                       "pPa", "pPb", or_t)

    def phE_back(i):
        c = ctx[i]
        s_r = ln_stats_back(c["t_stats2"], pPa[:, 0:MC], pPb[:, 0:MC], "pPa", "pPb")
        c["vh2"] = ln_apply(xmid, hb(i), s_r, M_CB1, M_SB)

    def phF(i):
        c = ctx[i]
        h = hb(i)
        for kc in range(8):
            eng = "V" if kc < 5 else "GP"
            w = []
            if eng == "GP" and kc == 5:
                w = [c["vh2"]]
            ew(eng, w, lambda e, _k=kc, _h=h: e.tensor_sub(
                dl_ap(_k, 0, 1), _h[:, _k * MC:_k * MC + 1], h2car[:, _k:_k + 1]))
            ew(eng, [], lambda e, _k=kc, _h=h: e.tensor_sub(
                dl_ap(_k, 1, MC), _h[:, _k * MC + 1:(_k + 1) * MC],
                _h[:, _k * MC:(_k + 1) * MC - 1]),
               inc=(eng if kc in (4, 7) else None))
        gp_dl = ("GP", CNT["GP"])
        for kc in range(8):
            w = [gp_dl] if kc == 5 else []
            ew("V", w, lambda e, _k=kc, _h=h: e.scalar_tensor_tensor(
                ufm_t[:, _k * MC:(_k + 1) * MC], dl_ap(_k, 0, MC),
                modv[:, M_MUR + _k:M_MUR + _k + 1], _h[:, _k * MC:(_k + 1) * MC],
                OP.mult, OP.add), inc=("V" if kc == 7 else None))
        c["v_xr"] = ("V", CNT["V"])
        ew("V", [gp_dl], lambda e, _h=h: e.tensor_copy(
            h2car[:, :].rearrange("p (c o) -> p c o", o=1),
            _h[:, :].rearrange("p (c t) -> p c t", c=8)[:, :, MC - 1:MC]))
        for kc in range(8):
            ew("V", [], lambda e, _k=kc, _h=h: e.scalar_tensor_tensor(
                utm_t[:, _k * MC:(_k + 1) * MC], dl_ap(_k, 0, MC),
                modv[:, M_MUK + _k:M_MUK + _k + 1], _h[:, _k * MC:(_k + 1) * MC],
                OP.mult, OP.add), inc=("V" if kc == 7 else None))
        c["v_xk"] = ("V", CNT["V"])

    def phG(i):
        c = ctx[i]
        spill_pO()
        xa = xin[i % 2]
        s_sigr = {}
        for m in range(8):
            bank, bkey = banks4[m % 4]
            extra = [c["v_xr"]] if m == 0 else []
            if i == 0 and m == 0:
                extra.append(wdone["wr"])

            def evac_r(t_end, bank, bkey, _m=m):
                s_c = ew("S", [("T", t_end)],
                         lambda e, _mm=_m, _b=bank: e.activation(
                             vtm_t[:, _mm * MC:(_mm + 1) * MC], _b[:, 0:MC],
                             AF.Sigmoid), inc="S")
                s_sigr[_m] = s_c
                RD[bkey] = [("S", s_c)]
            proj("wr", 1024, ufm_t, m, bank, bkey, extra, evac_r)

        v_kact_m = {}
        for m in range(8):
            bank, bkey = banks4[m % 4]
            extra = [c["v_xk"]] if m == 0 else []
            if i == 0 and m == 0:
                extra.append(wdone["wk2"])

            def evac_k2(t_end, bank, bkey, _m=m):
                sl = _m % 2
                w = [("T", t_end)]
                if _m >= 2:
                    w.append(("V", v_kact_m[_m - 2]))
                s_c = ew("S", w, lambda e, _b=bank, _s=sl: e.activation(
                    krl[:, _s * MC:(_s + 1) * MC], _b[:, 0:MC], AF.Relu), inc="S")
                v_c = ew("V", [("S", s_c)],
                         lambda e, _mm=_m, _b=bank, _s=sl: e.tensor_mul(
                             qf_t[:, _mm * MC:(_mm + 1) * MC], _b[:, 0:MC],
                             krl[:, _s * MC:(_s + 1) * MC]), inc="V")
                v_kact_m[_m] = v_c
                RD[bkey] = [("V", v_c)]
            proj("wk2", 1024, utm_t, m, bank, bkey, extra, evac_k2)
        v_kact = CNT["V"]

        for m in range(8):
            bank, bkey = banks4[m % 4]
            extra = [("V", v_kact), ("S", s_sigr[7])] if m == 0 else []
            if i == 0 and m == 0:
                extra.append(wdone["wv2"])

            def evac_v2(t_end, bank, bkey, _m=m):
                ew("V", [("T", t_end)], lambda e, _mm=_m, _b=bank: e.tensor_mul(
                    f0[:, :], _b[:, 0:MC], vtm_t[:, _mm * MC:(_mm + 1) * MC]))
                v_c = ew("V", [], lambda e, _mm=_m, _xa=xa: e.scalar_tensor_tensor(
                    _xa[:, _mm * MC:(_mm + 1) * MC], f0[:, :],
                    modv[:, M_GB + _mm:M_GB + _mm + 1],
                    xmid[:, _mm * MC:(_mm + 1) * MC], OP.mult, OP.add), inc="V")
                RD[bkey] = [("V", v_c)]
            proj("wv2", 1024, qf_t, m, bank, bkey, extra, evac_v2)
        c["t_v2"] = CNT["T"]
        v_final = CNT["V"]
        c["v_final"] = v_final

        ew("Y", [("V", v_final)],
           lambda e, _mc=i, _xa=xa: e.dma_start(
               out=dr["y"].rearrange("(c p) t -> p c t", p=128)[:, :, _mc * MC:(_mc + 1) * MC],
               in_=_xa[:, :].rearrange("p (c t) -> p c t", c=NCH)),
           inc="DO%d" % (i % 2), inc_by=16)
        if i + 2 < nmc:
            emit_xin(i + 2)

    phA_front(0)
    phA_back(0)
    phB1(0)
    phB2(0)
    phC(0)
    for mc in range(nmc):
        phD(mc)
        if attn_only:
            v_fin = ctx[mc]["v_resid"]
            ew("Y", [("V", v_fin)],
               lambda e, _mc=mc: e.dma_start(
                   out=dr["y"].rearrange("(c p) t -> p c t", p=128)[:, :, _mc * MC:(_mc + 1) * MC],
                   in_=xin[_mc % 2][:, :].rearrange("p (c t) -> p c t", c=NCH)),
               inc="DO%d" % (mc % 2), inc_by=16)
            if mc + 2 < nmc:
                emit_xin(mc + 2)
            if mc + 1 < nmc:
                phA_front(mc + 1)
                phA_back(mc + 1)
                phB1(mc + 1)
                phB2(mc + 1)
                phC(mc + 1)
            continue
        if mc + 1 < nmc:
            phA_front(mc + 1)
        phE_front(mc)
        if mc + 1 < nmc:
            phA_back(mc + 1)
        phE_back(mc)
        if mc + 1 < nmc:
            phB1(mc + 1)
        phF(mc)
        phG(mc)
        if mc + 1 < nmc:
            phB2(mc + 1)
            phC(mc + 1)

    n_do0 = (nmc + 1) // 2
    n_do1 = nmc // 2
    ew("G", [("DO0", 16 * n_do0), ("DO1", 16 * n_do1)], None)

    if dump:
        name2buf = {"h_t": h_t, "qt_t": qt_t, "qf_t": qf_t, "ufm_t": ufm_t,
                    "utm_t": utm_t, "vtm_t": vtm_t, "sg_t": sg_t, "or_t": or_t,
                    "atm_t": atm_t, "ssb_t": ssb_t, "xmid": xmid, "modv": modv,
                    "E_s": E_s, "ecar": ecar, "g1_t": g1_t, "sR": sR, "sM": sM,
                    "xin0": xin[0], "silc": silc, "silc_bf": silc_bf, "mtmp": mtmp,
                    "cv_s": cv_s, "sscr": sscr, "ones256": ones256}
        w = [("T", CNT["T"]), ("V", CNT["V"]), ("S", CNT["S"])]
        for di, nm in enumerate(dump):
            buf = name2buf[nm]
            dd = nc.dram_tensor("dbg_" + nm, list(buf.ap().shape),
                                buf.ap().dtype, kind="ExternalOutput").ap()
            ew("G", (w if di == 0 else []),
               lambda e, _d=dd, _b=buf: e.dma_start(out=_d, in_=_b[:, :]),
               inc="DO0", inc_by=16)
        ew("G", [("DO0", CNT["DO0"])], None)

    with nc.Block() as block:
        @block.tensor
        def _(e):
            for op in ST["T"]:
                op(e)

        @block.vector
        def _(e):
            for op in ST["V"]:
                op(e)

        @block.scalar
        def _(e):
            for op in ST["S"]:
                op(e)

        @block.sync
        def _(e):
            for op in ST["Y"]:
                op(e)

        @block.gpsimd
        def _(e):
            for op in ST["G"]:
                op(e)

    stack.close()
    return nc


_NC_CACHE = None


def kernel(**ins):
    global _NC_CACHE
    if _NC_CACHE is None:
        _NC_CACHE = build_nc()
    nc = _NC_CACHE

    def tr(a):
        return np.ascontiguousarray(np.asarray(a, dtype=np.float32).T)

    def vec8(v):
        return np.ascontiguousarray(np.asarray(v, np.float32).reshape(8, 128).T)

    x = np.asarray(ins["L0_attn_hidden_states"], np.float32)
    shared = {
        "wq": tr(ins["L0_attn_Wq"]),
        "wk": tr(ins["L0_attn_Wk"]),
        "wv": tr(ins["L0_attn_Wv"]),
        "wg1": tr(ins["L0_attn_Wg1"]),
        "wg2": tr(ins["L0_attn_Wg2"]),
        "wo": tr(ins["L0_attn_Wo"]),
        "wr": tr(ins["L0_ffn_W_r"]),
        "wk2": tr(ins["L0_ffn_W_k"]),
        "wv2": tr(ins["L0_ffn_W_v"]),
        "wmod": np.ascontiguousarray(np.asarray(ins["L0_wmod"], np.float32)),
        "gnw": vec8(ins["L0_attn_g_norm_weight"]),
        "mur": vec8(ins["L0_ffn_mu_r"]),
        "muk": vec8(ins["L0_ffn_mu_k"]),
        "msk": np.ascontiguousarray(np.triu(np.ones((128, 128), np.float32))),
    }
    in_maps = []
    for b in range(8):
        m = dict(shared)
        m["x"] = tr(x[b])
        m["cvec"] = vec8(np.asarray(ins["L0_c"], np.float32)[b])
        in_maps.append(m)

    res = run_bass_kernel_spmd(nc, in_maps, core_ids=list(range(8)))
    out = np.stack([np.ascontiguousarray(np.asarray(r["y"], np.float32).T)
                    for r in res.results], axis=0)
    return out
